# version 25
# baseline (speedup 1.0000x reference)
import numpy as np
import concourse.bass as bass
import concourse.mybir as mybir
from concourse.bass_utils import run_bass_kernel_spmd

F32 = mybir.dt.float32
BF16 = mybir.dt.bfloat16
I32 = mybir.dt.int32
U8 = mybir.dt.uint8
AF = mybir.ActivationFunctionType
ALU = mybir.AluOpType
AXX = mybir.AxisListType.X

KB = 1024
N_CORES = 8
B, T, C = 4, 4096, 1024
H = 16
M = 2048
NTC = 16
F = 4096
EPS = 1e-5


def head_col(h):
    return (h // 7) * 512 + (h % 7) * 65


QROT = 8
SAFE_EDGES = True


class Op:
    __slots__ = ("eng", "fn", "reads", "writes", "queue", "idx", "inc",
                 "inc_val", "waits", "deps", "qsem", "issue_wait", "cont",
                 "guar")

    def __init__(self, eng, fn, reads, writes, queue, cont=False):
        self.eng = eng
        self.fn = fn
        self.reads = tuple(reads)
        self.writes = tuple(writes)
        self.queue = queue
        self.inc = False
        self.inc_val = None
        self.waits = []
        self.deps = []
        self.qsem = None
        self.issue_wait = None
        self.cont = cont
        self.guar = {}


class Rec:
    def __init__(self, dma_counts=None):
        self.ops = []
        self.dma_counts = dma_counts

    def add(self, eng, fn, reads=(), writes=(), dma=False, cont=False):
        self.ops.append(Op(eng, fn, reads, writes,
                           eng + "q" if dma else None, cont))

    def resolve(self):
        ops = self.ops
        for i, op in enumerate(ops):
            op.idx = i
        quse = {}
        scount = {}
        for op in ops:
            if op.queue is not None:
                op.inc = True
                i = quse.get(op.queue, 0)
                quse[op.queue] = i + 1
                skey = f"{op.queue}{i % QROT}"
                op.qsem = skey
                prev = scount.get(skey, 0)
                op.issue_wait = (skey, prev) if prev else None
                k = self.dma_counts.get(op.idx, 1) if self.dma_counts else 1
                scount[skey] = prev + 16 * k
                op.inc_val = scount[skey]
        writer = {}
        readers = {}
        for op in ops:
            deps = {}

            def dep_on(d, op=op, deps=deps):
                if d is None or d is op:
                    return
                src = d.qsem if d.queue else d.eng
                if d.queue is None and op.queue is not None and d.eng == op.eng:
                    return
                if (d.queue is None and op.queue is None
                        and d.eng == "T" and op.eng == "T" and op.cont):
                    return
                if (not SAFE_EDGES and d.queue is None and op.queue is None
                        and d.eng == op.eng and d.eng in ("V", "S")):
                    return
                cur = deps.get(src)
                if cur is None or d.idx > cur.idx:
                    deps[src] = d

            for b_ in op.reads:
                dep_on(writer.get(b_))
            for b_ in op.writes:
                dep_on(writer.get(b_))
                for rd in readers.get(b_, {}).values():
                    dep_on(rd)
            op.deps = list(deps.values())
            for d in op.deps:
                d.inc = True
            for b_ in op.reads:
                readers.setdefault(b_, {})[op.queue or op.eng] = op
            for b_ in op.writes:
                writer[b_] = op
                readers[b_] = {}
        count = {}
        for op in ops:
            if op.queue is None and op.inc:
                key = op.eng
                count[key] = count.get(key, 0) + 1
                op.inc_val = count[key]
        stream_cur = {}
        for op in ops:
            me = op.eng
            cur = stream_cur.setdefault(me, {})
            for d in sorted(op.deps, key=lambda d: -d.idx):
                key = d.qsem if d.queue else d.eng
                if cur.get(key, 0) < d.inc_val:
                    op.waits.append((key, d.inc_val))
                    cur[key] = d.inc_val
                for k2, v2 in d.guar.items():
                    if cur.get(k2, 0) < v2:
                        cur[k2] = v2
            g = dict(cur)
            if op.inc:
                k = op.qsem if op.queue else op.eng
                g[k] = max(g.get(k, 0), op.inc_val)
            op.guar = g
        return ops


def build_nc():
    counts = _build_inner(None)
    return _build_inner(counts)


def _build_inner(dma_counts):
    nc = bass.Bass("TRN2", target_bir_lowering=False, debug=False)

    x_d = nc.dram_tensor("x", [M, C], F32, kind="ExternalInput")
    xT_d = nc.dram_tensor("xT", [C, M], F32, kind="ExternalInput")
    wkT_d = nc.dram_tensor("wkT", [C, C], F32, kind="ExternalInput")
    wvT_d = nc.dram_tensor("wvT", [C, C], F32, kind="ExternalInput")
    wqT_d = nc.dram_tensor("wqT", [C, C], F32, kind="ExternalInput")
    woT_d = nc.dram_tensor("woT", [C, C], F32, kind="ExternalInput")
    brows_d = nc.dram_tensor("brows", [5, C], F32, kind="ExternalInput")
    gamf_d = nc.dram_tensor("gamf", [128, 8], F32, kind="ExternalInput")
    betf_d = nc.dram_tensor("betf", [128, 8], F32, kind="ExternalInput")
    b1f_d = nc.dram_tensor("b1f", [128, 32], F32, kind="ExternalInput")
    bqf_d = nc.dram_tensor("bqf", [128, 8], F32, kind="ExternalInput")
    q1_d = nc.dram_tensor("q1", [64 * 128, 512], U8, kind="ExternalInput")
    s1_d = nc.dram_tensor("s1", [8, F], F32, kind="ExternalInput")
    q2_d = nc.dram_tensor("q2", [64 * 128, 512], U8, kind="ExternalInput")
    s2_d = nc.dram_tensor("s2", [32, C], F32, kind="ExternalInput")
    y_d = nc.dram_tensor("y", [M, C], F32, kind="ExternalOutput")

    x1_d = nc.dram_tensor("x1spill", [M, C], F32)
    cc_in = nc.dram_tensor("cc_in", [64, 4160], F32)
    cc_out = nc.dram_tensor("cc_out", [64, 4160], F32)

    A = nc.alloc_sbuf_tensor_at
    O = 16 * KB
    ident = A("ident", [128, 128], BF16, offset=O + 1024)
    ones = A("ones", [1, 512], BF16, offset=O + 1280)
    gamf = A("gamf_s", [128, 8], F32, offset=O + 2304)
    betf = A("betf_s", [128, 8], F32, offset=O + 2336)
    b1f = A("b1f_s", [128, 32], F32, offset=O + 2368)
    stat = A("stat", [128, 64], F32, offset=O + 2496)
    bqf = A("bqf_s", [128, 8], F32, offset=O + 3072)
    ss = A("ss", [128, 16], F32, offset=O + 2752)
    vb = A("vb", [128, 16], F32, offset=O + 2816)
    epsc = A("epsc", [128, 4], F32, offset=O + 2880)
    den = A("den", [128, 16], F32, offset=O + 2944)
    rr = A("rr", [128, 16], F32, offset=O + 3008)
    b2rows = A("b2rows", [1, 4, 1024], BF16, offset=20 * KB)
    b1rows = A("b1rows", [1, 6, 1024], BF16, offset=28 * KB)
    qst = A("qst", [128, 4, 512], U8, offset=28 * KB)
    sbc = A("sbc", [128, 4, 512], F32, offset=30 * KB)
    u_t2 = A("u_t2", [128, 1024], F32, offset=152 * KB)
    xnt = A("xnt", [128, 1024], BF16, offset=38 * KB)
    wstC = A("wstC", [128, 1024], BF16, offset=129 * KB)
    wstD = A("wstD", [128, 1024], BF16, offset=156 * KB)
    xld2 = A("xld2", [128, 1024], F32, offset=131 * KB)
    ofm2 = A("ofm2", [128, 1024], BF16, offset=22 * KB)
    otm2 = A("otm2", [128, 1024], BF16, offset=26 * KB)
    t1q = A("t1q", [128, 1024], F32, offset=72 * KB)
    t2q = A("t2q", [128, 1024], F32, offset=76 * KB)
    stgB = A("stgB", [128, 2048], F32, offset=48 * KB)
    x1c = A("x1c", [128, 5, 1024], F32, offset=136 * KB)
    stgA = A("stgA", [128, 2048], F32, offset=40 * KB)
    stgC = A("stgC", [128, 2048], F32, offset=72 * KB)
    stgD = A("stgD", [128, 2048], F32, offset=80 * KB)
    wsg = A("wsg", [128, 4, 1024], F32, offset=159 * KB)
    brf = A("brf", [1, 1024], F32, offset=100 * KB)
    t1 = A("t1", [128, 1024], F32, offset=48 * KB)
    t2 = A("t2", [128, 1024], F32, offset=52 * KB)
    wkb = A("wkb", [128, 8, 1024], BF16, offset=56 * KB)
    wqb = A("wqb", [128, 8, 1024], BF16, offset=56 * KB)
    ktiles = A("ktiles", [128, 4, 1024], BF16, offset=72 * KB)
    v_aug = A("v_aug", [128, 4, 1040], BF16, offset=80 * KB)
    kv_stage = A("kv_stage", [64, 2, 1040], F32, offset=88 * KB + 256)
    w1b = A("w1b", [128, 8, 4096], BF16, offset=40 * KB)
    x_fm = A("x_fm", [128, 8, 2048], BF16, offset=104 * KB)
    kvr = A("kvr", [64, 4160], F32, offset=104 * KB)
    xld = A("xld", [128, 1024], F32, offset=104 * KB)
    ofm = A("ofm", [128, 1024], BF16, offset=108 * KB)
    otm = A("otm", [128, 1024], BF16, offset=110 * KB)
    u_t = A("u_t", [128, 1024], F32, offset=112 * KB)
    x1t = A("x1t", [128, 1024], F32, offset=116 * KB)
    kv_aug = A("kv_aug", [128, 4160], BF16, offset=120 * KB + 512)
    y_fm = A("y_fm", [128, 32, 512], BF16, offset=104 * KB)
    h_fm = A("h_fm", [128, 8, 512], BF16, offset=28 * KB)
    wvb = A("wvb", [128, 8, 1024], BF16, offset=136 * KB)
    wob = A("wob", [128, 8, 1024], BF16, offset=136 * KB)
    w2b = A("w2b", [128, 32, 1024], BF16, offset=159 * KB)
    q_fm = A("q_fm", [128, 8, 2048], BF16, offset=191 * KB)
    bk_bc = A("bk_bc", [128, 1024], F32, offset=191 * KB)
    bv_bc = A("bv_bc", [128, 1024], F32, offset=195 * KB)
    t0p = A("t0p", [128, 1024], F32, offset=199 * KB)
    t0q = A("t0q", [128, 1024], F32, offset=48 * KB)

    psA = nc.alloc_psum_tensor("psA", [128, 1024], F32)
    psB = nc.alloc_psum_tensor("psB", [128, 1024], F32)
    psN = nc.alloc_psum_tensor("psN", [128, 1536], F32)
    psT = nc.alloc_psum_tensor("psT", [128, 1024], BF16)

    sem_keys = ["T", "V", "S", "P", "Y"]
    for q in ["Yq", "Sq", "Pq"]:
        sem_keys += [f"{q}{i}" for i in range(QROT)]
    sems = {k: nc.alloc_semaphore("sem_" + k) for k in sem_keys}

    r = Rec(dma_counts)
    T_ = lambda fn, reads=(), writes=(), cont=False: r.add("T", fn, reads,
                                                           writes, cont=cont)
    V_ = lambda fn, reads=(), writes=(): r.add("V", fn, reads, writes)
    S_ = lambda fn, reads=(), writes=(): r.add("S", fn, reads, writes)
    P_ = lambda fn, reads=(), writes=(): r.add("P", fn, reads, writes)
    Yd = lambda fn, reads=(), writes=(): r.add("Y", fn, reads, writes, dma=True)
    Sd = lambda fn, reads=(), writes=(): r.add("S", fn, reads, writes, dma=True)
    Pd = lambda fn, reads=(), writes=(): r.add("P", fn, reads, writes, dma=True)

    XFM = [f"xfm{c}" for c in range(8)]
    VAK = [f"va{i}" for i in range(4)]
    KVS = ["kvs0", "kvs1"]
    WOK = [f"wo{c}" for c in range(8)]

    P_(lambda g: g.memset(ident[:, :], 0.0), writes=["ident", "ptok"])
    P_(lambda g: g.affine_select(out=ident[:, :], in_=ident[:, :],
                                 compare_op=ALU.not_equal, fill=1.0, base=0,
                                 pattern=[[-1, 128]], channel_multiplier=1),
       reads=["ident", "ptok"], writes=["ident", "ptok"])
    P_(lambda g: g.memset(ones[:, :], 1.0), writes=["ones", "ptok"])
    P_(lambda g: g.memset(epsc[:, 0:1], EPS), writes=["epsc", "ptok"])
    P_(lambda g: g.memset(epsc[:, 1:2], EPS / 4.0), writes=["epsc", "ptok"])
    P_(lambda g: g.memset(epsc[:, 2:3], -128.0), writes=["epsc", "ptok"])
    P_(lambda g: g.memset(stat[:, 32:64], 1024.0),
       writes=["ptok"] + [f"sa{p}{c}" for p in range(2) for c in "0123"])

    XSTG = [stgA, stgB, stgC, stgD]
    for c in range(8):
        stg, skey = XSTG[c % 4], f"xsg{c % 4}"
        Yd(lambda e, c=c, stg=stg: e.dma_start(
            out=stg[:, :], in_=xT_d[c * 128:(c + 1) * 128, :]),
           reads=[], writes=[skey])
        V_(lambda v, c=c, stg=stg: v.tensor_copy(out=x_fm[:, c, :],
                                                 in_=stg[:, :]),
           reads=[skey], writes=[f"xfm{c}"])
    def load_w(dram, dst, dstkey, alias=None):
        for c in range(8):
            slot = c % 4
            Pd(lambda e, c=c, slot=slot, dram=dram: e.dma_start(
                out=wsg[:, slot, :], in_=dram[c * 128:(c + 1) * 128, :]),
               writes=[f"wsg{slot}"])
            al = [f"{alias}{c}"] if alias else []
            S_(lambda sc, c=c, slot=slot, dst=dst: sc.copy(
                out=dst[:, c, :], in_=wsg[:, slot, :]),
               reads=[f"wsg{slot}"], writes=[f"{dstkey}{c}"] + al)

    load_w(wkT_d, wkb, "wk")
    load_w(wvT_d, wvb, "wv")
    P_(lambda g: g.memset(v_aug[:, :, :], 1.0),
       writes=VAK + ["ptok", "xsg3"])

    Yd(lambda e: e.dma_start(out=gamf[:, :], in_=gamf_d[:, :]), writes=["gamf"])
    Yd(lambda e: e.dma_start(out=betf[:, :], in_=betf_d[:, :]), writes=["betf"])
    Yd(lambda e: e.dma_start(out=b1f[:, :], in_=b1f_d[:, :]), writes=["b1f"])
    Yd(lambda e: e.dma_start(out=bqf[:, :], in_=bqf_d[:, :]), writes=["bqf"])

    for row, tgt, slot, nm in [(0, b1rows, 0, "bk"), (1, b1rows, 2, "bv"),
                               (3, b2rows, 0, "bo"),
                               (4, b2rows, 2, "b2")]:
        hi = nm + "hi"
        Yd(lambda e, row=row: e.dma_start(out=brf[:, :],
                                          in_=brows_d[row:row + 1, :]),
           writes=["brf"])
        S_(lambda sc, tgt=tgt, slot=slot: sc.copy(
            out=tgt[:, slot, :], in_=brf[:, :]),
           reads=["brf"], writes=[hi])

    for half in range(2):
        sl = slice(half * 512, (half + 1) * 512)
        T_(lambda t, sl=sl: t.matmul(psA[:, sl], ones[0:1, 0:128],
                                     b1rows[0:1, 0, sl], start=True, stop=True),
           reads=["ones", "bkhi"], writes=["psA0" if half == 0 else "psA1"])
        T_(lambda t, sl=sl: t.matmul(psB[:, sl], ones[0:1, 0:128],
                                     b1rows[0:1, 2, sl], start=True, stop=True),
           reads=["ones", "bvhi"], writes=["psB0" if half == 0 else "psB1"])
    V_(lambda v: v.tensor_copy(out=bk_bc[:, :], in_=psA[:, :]),
       reads=["psA0", "psA1"], writes=["bkbc"])
    V_(lambda v: v.tensor_copy(out=bv_bc[:, :], in_=psB[:, :]),
       reads=["psB0", "psB1"], writes=["bvbc"])

    for b in range(B):
        for i in range(4):
            tc = 4 * b + i
            for half in range(2):
                pk = f"psA{half}"
                sl = slice(half * 512, (half + 1) * 512)
                for kc in range(8):
                    T_(lambda t, kc=kc, sl=sl, tc=tc: t.matmul(
                        psA[:, sl], x_fm[:, kc, tc * 128:(tc + 1) * 128],
                        wkb[:, kc, sl], start=(kc == 0), stop=(kc == 7)),
                       reads=[f"xfm{kc}", f"wk{kc}"], writes=[pk],
                       cont=(kc != 0))
            kal = ["xsg1"] if tc == 0 else []
            V_(lambda v: v.tensor_tensor(out=t0p[:, :], in0=psA[:, :],
                                         in1=bk_bc[:, :], op=ALU.add),
               reads=["psA0", "psA1", "bkbc"], writes=["t0"])
            S_(lambda sc: sc.activation(out=t1[:, :], in_=t0p[:, :],
                                        func=AF.Relu, scale=-1.0),
               reads=["t0"], writes=["t1"] + kal)
            S_(lambda sc: sc.activation(out=t2[:, :], in_=t1[:, :],
                                        func=AF.Exp, scale=-1.0),
               reads=["t1"], writes=["t2"] + kal)
            V_(lambda v, i=i: v.scalar_tensor_tensor(
                out=ktiles[:, i, :], in0=t0p[:, :], scalar=0.0,
                in1=t2[:, :], op0=ALU.max, op1=ALU.add),
               reads=["t0", "t2"],
               writes=[f"kt{i}"] + (["xsg2"] if tc < 4 else []))
            for half in range(2):
                pk = f"psB{half}"
                sl = slice(half * 512, (half + 1) * 512)
                for kc in range(8):
                    T_(lambda t, kc=kc, sl=sl, tc=tc: t.matmul(
                        psB[:, sl], x_fm[:, kc, tc * 128:(tc + 1) * 128],
                        wvb[:, kc, sl], start=(kc == 0), stop=(kc == 7)),
                       reads=[f"xfm{kc}", f"wv{kc}"], writes=[pk],
                       cont=(kc != 0))
            V_(lambda v: v.tensor_tensor(out=t0p[:, :], in0=psB[:, :],
                                         in1=bv_bc[:, :], op=ALU.add),
               reads=["psB0", "psB1", "bvbc"], writes=["t0"])
            S_(lambda sc, i=i: sc.activation(
                out=v_aug[:, i, :].rearrange("p (h e) -> p h e", h=16)[:, :, 0:64],
                in_=t0p[:, :].rearrange("p (h e) -> p h e", h=16),
                func=AF.Identity),
               reads=["t0"], writes=[f"va{i}"])
        for h in range(H):
            col = head_col(h)
            for i in range(4):
                T_(lambda t, h=h, i=i, col=col: t.matmul(
                    psN[0:64, col:col + 65],
                    ktiles[:, i, h * 64:(h + 1) * 64],
                    v_aug[:, i, h * 65:(h + 1) * 65],
                    start=(i == 0), stop=(i == 3)),
                   reads=[f"kt{i}", f"va{i}"], writes=[f"psN{h // 7}"],
                   cont=(i != 0))
        buf = b % 2
        for j, nh in [(0, 7), (1, 7), (2, 2)]:
            V_(lambda v, buf=buf, j=j, nh=nh: v.tensor_copy(
                out=kv_stage[:, buf, :].rearrange(
                    "p (h e) -> p h e", h=16)[:, 7 * j:7 * j + nh, :],
                in_=psN[0:64, j * 512:j * 512 + nh * 65].rearrange(
                    "p (h e) -> p h e", h=nh)),
               reads=[f"psN{j}"], writes=[f"kvs{buf}"])
        Yd(lambda e, b=b, buf=buf: e.dma_start(
            out=cc_in[:, b * 1040:(b + 1) * 1040],
            in_=kv_stage[:, buf, :]),
           reads=[f"kvs{buf}"], writes=[f"ccin{b}"])

    KTK = [f"kt{i}" for i in range(4)]
    W1AL = {0: ["xsg0"], 1: ["t1", "t2", "xsg1", "t0q"],
            2: ["wq0", "wq1", "wq2", "wq3"], 3: ["wq4", "wq5", "wq6", "wq7"],
            4: KTK + ["t1q", "t2q", "xsg2"], 5: VAK + ["xsg3"],
            6: VAK + KVS, 7: KVS}

    def dq_aps(job):
        kind, ch, s = job
        if kind == "w1":
            qsrc = q1_d[(ch * 8 + s) * 128:(ch * 8 + s + 1) * 128, :]
            ssrc = s1_d[ch:ch + 1, s * 512:(s + 1) * 512]
            dst = w1b[:, ch, s * 512:(s + 1) * 512]
            dkey = f"w1b{ch}"
            al = W1AL.get(ch, []) if s == 0 else []
        else:
            qsrc = q2_d[(ch * 2 + s) * 128:(ch * 2 + s + 1) * 128, :]
            ssrc = s2_d[ch:ch + 1, s * 512:(s + 1) * 512]
            dst = w2b[:, ch, s * 512:(s + 1) * 512]
            dkey = f"w2b{ch}"
            al = ([f"qf{(ch - 16) // 2}_{(ch - 16) % 2}"]
                  if (ch >= 16 and s == 0) else
                  [f"wsg{ch // 2}"] if (ch < 8 and s == 0) else [])
        return qsrc, ssrc, dst, dkey, al

    def emit_dq_dma(job, slot, first=False):
        qsrc, ssrc, dst, dkey, al = dq_aps(job)
        qk, sk = f"qst{slot}", f"sbc{slot}"
        qal = ["bkhi"] if first else []
        sal = ["bvhi"] if first else []
        Yd(lambda e, qsrc=qsrc, slot=slot: e.dma_start(
            out=qst[:, slot, :], in_=qsrc), writes=[qk] + qal)
        Yd(lambda e, ssrc=ssrc, slot=slot: e.dma_start(
            out=sbc[:, slot, :], in_=ssrc.partition_broadcast(128)),
           writes=[sk] + sal)

    def emit_dq_comp(job, slot):
        qsrc, ssrc, dst, dkey, al = dq_aps(job)
        qk, sk = f"qst{slot}", f"sbc{slot}"
        V_(lambda v, dst=dst, slot=slot: v.scalar_tensor_tensor(
            out=dst, in0=qst[:, slot, :], scalar=128.0,
            in1=sbc[:, slot, :], op0=ALU.subtract, op1=ALU.mult),
           reads=[qk, sk], writes=[dkey] + al)

    dq_p2 = ([("w1", kc, s) for kc in (5, 6, 7, 0) for s in range(8)] +
             [("w2", fc, s) for fc in range(8, 16) for s in range(2)])
    dq_c = ([("w1", kc, s) for kc in (1, 2, 3, 4) for s in range(8)] +
            [("w2", fc, s) for fc in range(8) for s in range(2)] +
            [("w2", fc, s) for fc in range(16, 32, 2) for s in range(2)] +
            [("w2", fc, s) for fc in range(17, 32, 2) for s in range(2)])
    dq_gates = ([0] * 48 + [9] * 16 + [16] * 16)
    dq_state = {"nd": 0, "nc": 0, "tot0": 0, "jobs": dq_p2,
                "gates": [0] * len(dq_p2)}

    def dq_switch(jobs, gates=None):
        st = dq_state
        assert st["nd"] == st["nc"] == len(st["jobs"])
        st["tot0"] += len(st["jobs"])
        st["jobs"] = jobs
        st["gates"] = gates if gates is not None else [0] * len(jobs)
        st["nd"] = st["nc"] = 0

    def pump_dma(k, it=10 ** 9):
        st = dq_state
        while (k > 0 and st["nd"] < len(st["jobs"])
               and st["nd"] < st["nc"] + 4
               and st["gates"][st["nd"]] <= it):
            j = st["nd"]
            emit_dq_dma(st["jobs"][j], (st["tot0"] + j) % 4,
                        first=(st["tot0"] + j == 0))
            st["nd"] += 1
            k -= 1

    def pump_comp(k):
        st = dq_state
        while k > 0 and st["nc"] < min(st["nd"], len(st["jobs"])):
            j = st["nc"]
            emit_dq_comp(st["jobs"][j], (st["tot0"] + j) % 4)
            st["nc"] += 1
            k -= 1

    def pump(k):
        pump_dma(k)
        pump_comp(k)

    P_(lambda g: g.collective_compute(
        "AllReduce", ALU.add, replica_groups=[list(range(N_CORES))],
        ins=[cc_in[:, :]], outs=[cc_out[:, :]]),
       reads=[f"ccin{b_}" for b_ in range(B)] + ["ptok"],
       writes=["ccout", "ptok"])

    load_w(wqT_d, wqb, "wq", alias="wk")
    load_w(woT_d, wob, "wo", alias="wv")

    for nch in range(8):
        for strip in range(2):
            ps, pks = (psA, ["psA0", "psA1"]) if (nch * 2 + strip) % 2 == 0 \
                else (psB, ["psB0", "psB1"])
            for half in range(2):
                sl = slice(half * 512, (half + 1) * 512)
                t0 = strip * 1024 + half * 512
                for kc in range(8):
                    T_(lambda t, ps=ps, nch=nch, kc=kc, sl=sl, t0=t0: t.matmul(
                        ps[:, sl], wqb[:, kc, nch * 128:(nch + 1) * 128],
                        x_fm[:, kc, t0:t0 + 512],
                        start=(kc == 0), stop=(kc == 7)),
                       reads=[f"wq{kc}", f"xfm{kc}"], writes=[pks[half]],
                       cont=(kc != 0))
            qa1 = ["kt0", "kt1"] if (nch == 0 and strip == 0) else []
            qa2 = ["kt2", "kt3"] if (nch == 0 and strip == 0) else []
            qa0 = (["t1", "xsg1"] if (nch == 0 and strip == 0) else [])
            V_(lambda v, ps=ps, nch=nch: v.tensor_scalar_add(
                out=t0q[:, :], in0=ps[:, :], scalar1=bqf[:, nch:nch + 1]),
               reads=pks + ["bqf"], writes=["t0q"] + qa0)
            S_(lambda sc: sc.activation(out=t1q[:, :], in_=t0q[:, :],
                                        func=AF.Relu, scale=-1.0),
               reads=["t0q"], writes=["t1q"] + qa1)
            S_(lambda sc: sc.activation(out=t2q[:, :], in_=t1q[:, :],
                                        func=AF.Exp, scale=-1.0),
               reads=["t1q"], writes=["t2q"] + qa2)
            V_(lambda v, nch=nch, strip=strip: v.scalar_tensor_tensor(
                out=q_fm[:, nch, strip * 1024:(strip + 1) * 1024],
                in0=t0q[:, :], scalar=0.0, in1=t2q[:, :],
                op0=ALU.max, op1=ALU.add),
               reads=["t0q", "t2q"],
               writes=[f"qf{nch}_{strip}"] +
                      (["bkbc", "bvbc"] if (nch == 0 and strip == 0) else []) +
                      (["t0"] if (nch == 2 and strip == 0) else []))
            pump_dma(2)
            pump_comp(2)

    while dq_state["nc"] < len(dq_state["jobs"]):
        pump_dma(2)
        pump_comp(2)
    dq_switch(dq_c, dq_gates)

    Yd(lambda e: e.dma_start(out=kvr[:, :], in_=cc_out[:, :]),
       reads=["ccout"], writes=["kvr"] + XFM)
    V_(lambda v: v.tensor_copy(out=kv_aug[0:64, :], in_=kvr[:, :]),
       reads=["kvr"], writes=["kva_lo"])
    Pd(lambda e: e.dma_start(out=kv_aug[64:128, :], in_=kv_aug[0:64, :]),
       reads=["kva_lo"], writes=["kva_hi"])

    QSB = [f"qst{j}" for j in range(4)] + [f"sbc{j}" for j in range(3)]
    Y0AL = ["xld0", "ofm0", "otm0", "u0", "x1t", "kva_lo", "kva_hi",
            "wstC", "xld1"]
    X1AL = {0: ["wo0", "wo1"], 1: ["wo2", "wo3"], 2: ["wo4", "wo5"],
            3: ["wo6", "wo7"], 4: ["u1"]}

    def load_tc(tc):
        if not 0 <= tc < NTC:
            return
        sl5 = tc % 5
        x1al = X1AL.get(sl5, []) if tc < 5 else []
        Pd(lambda e, tc=tc, sl5=sl5: e.dma_start(
            out=x1c[:, sl5, :], in_=x1_d[tc * 128:(tc + 1) * 128, :]),
           reads=[f"x1d{tc}"], writes=[f"x1c{sl5}"] + x1al)

    def emit_h_tc(tc):
        if not 0 <= tc < NTC:
            return
        sl5 = tc % 5
        q = tc % 4
        pos = q * 128
        V_(lambda v, tc=tc, sl5=sl5: v.tensor_scalar(
            out=xnt[:, :], in0=x1c[:, sl5, :],
            scalar1=stat[:, 2 * tc:2 * tc + 1],
            scalar2=stat[:, 2 * tc + 1:2 * tc + 2],
            op0=ALU.subtract, op1=ALU.mult),
           reads=[f"x1c{sl5}", f"mr_{tc}"], writes=["xnt"])
        for kc in range(8):
            T_(lambda t, kc=kc: t.transpose(
                psT[:, kc * 128:(kc + 1) * 128],
                xnt[:, kc * 128:(kc + 1) * 128], ident[:, :]),
               reads=["xnt", "ident"], writes=["psT"])
        hal = QSB if tc == 0 else []
        for kc in range(8):
            V_(lambda v, kc=kc, pos=pos, q=q: v.tensor_scalar(
                out=h_fm[:, kc, pos:pos + 128],
                in0=psT[:, kc * 128:(kc + 1) * 128],
                scalar1=gamf[:, kc:kc + 1], scalar2=betf[:, kc:kc + 1],
                op0=ALU.mult, op1=ALU.add),
               reads=["psT", "gamf", "betf"],
               writes=[f"h{kc}_{q}"] + (hal if kc == 0 else []))


    def stage_f_tail(k):
        ps, pks = (psA, ["psA0", "psA1"]) if k % 2 == 0 \
            else (psB, ["psB0", "psB1"])
        p = k % 2
        B0 = 32 + 16 * p
        V_(lambda v, k=k, ps=ps, p=p: v.tensor_tensor(
            out=uB[p][:, :], in0=ps[:, :], in1=xldB[p][:, :], op=ALU.add),
           reads=pks + [f"xld{p}"],
           writes=[f"u{p}"] + (["kva_lo", "kva_hi", "t0"]
                                if k == 0 else []))
        V_(lambda v, p=p, B0=B0: v.reduce_sum(out=stat[:, B0:B0 + 1],
                                              in_=uB[p][:, :], axis=AXX),
           reads=[f"u{p}"], writes=[f"sa{p}0"])
        S_(lambda sc, p=p, B0=B0: sc.activation(
            out=wstC[:, :], in_=uB[p][:, :], func=AF.Square,
            accum_out=stat[:, B0 + 2:B0 + 3]),
           reads=[f"u{p}"], writes=["wstC", f"sa{p}2"] +
                                   (["xfm6"] if k == 0 else []))

    def stage_h(m):
        if True:
            q = m % 2
            B0 = 32 + 16 * q
            q2 = (m + 1) % 2
            B2 = 32 + 16 * q2
            V_(lambda v, q=q, B0=B0: v.tensor_scalar(
                out=x1t[:, :], in0=uB[q][:, :],
                scalar1=stat[:, B0 + 4:B0 + 5],
                scalar2=stat[:, B0 + 12:B0 + 13],
                op0=ALU.subtract, op1=ALU.mult),
               reads=[f"u{q}", f"sa{q}m", f"sa{q}r"], writes=["x1t"])
            S_(lambda sc, B2=B2: sc.activation(out=wstC[:, :], in_=x1t[:, :],
                                               func=AF.Identity,
                                               accum_out=stat[:, B2 + 1:B2 + 2]),
               reads=["x1t"], writes=["wstC", f"sa{q2}1"])
            S_(lambda sc, B2=B2: sc.activation(out=wstC[:, :], in_=x1t[:, :],
                                               func=AF.Square,
                                               accum_out=stat[:, B2 + 3:B2 + 4]),
               reads=["x1t"], writes=["wstC", f"sa{q2}3"])
            Pd(lambda e, m=m: e.dma_start(out=x1_d[m * 128:(m + 1) * 128, :],
                                          in_=x1t[:, :]),
               reads=["x1t"], writes=[f"x1d{m}"])

    def stage_j(kc2):
        if True:
            pc = kc2 % 2
            Bc = 32 + 16 * pc
            racc = ([f"sa{pc}0", f"sa{pc}1"] if kc2 < NTC else [f"sa{pc}1"])
            S_(lambda sc, Bc=Bc: sc.activation(
                out=stat[:, Bc + 4:Bc + 6], in_=stat[:, Bc:Bc + 2],
                func=AF.Identity, scale=1.0 / C),
               reads=racc, writes=[f"sa{pc}m"])
            S_(lambda sc, Bc=Bc: sc.activation(
                out=stat[:, Bc + 6:Bc + 8], in_=stat[:, Bc + 4:Bc + 6],
                func=AF.Square),
               reads=[f"sa{pc}m"], writes=[f"sa{pc}q"])
            V_(lambda v, Bc=Bc: v.scalar_tensor_tensor(
                out=stat[:, Bc + 8:Bc + 10], in0=stat[:, Bc + 2:Bc + 4],
                scalar=1.0 / C, in1=stat[:, Bc + 6:Bc + 8],
                op0=ALU.mult, op1=ALU.subtract),
               reads=[f"sa{pc}q", f"sa{pc}2", f"sa{pc}3"], writes=[f"sa{pc}n"])
            S_(lambda sc, Bc=Bc: sc.activation(
                out=stat[:, Bc + 10:Bc + 12], in_=stat[:, Bc + 8:Bc + 10],
                func=AF.Sqrt, bias=epsc[:, 0:1]),
               reads=[f"sa{pc}n", "epsc"], writes=[f"sa{pc}s"])
            V_(lambda v, Bc=Bc: v.reciprocal(
                out=stat[:, Bc + 12:Bc + 14], in_=stat[:, Bc + 10:Bc + 12]),
               reads=[f"sa{pc}s"], writes=[f"sa{pc}r"])
            mp = kc2 - 1
            if mp >= 0:
                V_(lambda v, Bc=Bc, mp=mp: v.tensor_copy(
                    out=stat[:, 2 * mp:2 * mp + 1], in_=stat[:, Bc + 5:Bc + 6]),
                   reads=[f"sa{pc}m"], writes=[f"mr_{mp}"])
                V_(lambda v, Bc=Bc, mp=mp: v.tensor_copy(
                    out=stat[:, 2 * mp + 1:2 * mp + 2],
                    in_=stat[:, Bc + 13:Bc + 14]),
                   reads=[f"sa{pc}r"], writes=[f"mr_{mp}"])

    uB = [u_t, u_t2]
    xldB = [xld, xld2]
    otmB = [otm, otm2]
    ofmB = [ofm, ofm2]

    for i in range(NTC + 6):
        pump_dma(5, i)
        jx = i - 2
        if 0 <= jx < NTC:
            xal = (["kvr"] if jx == 0 else []) + (["otm2"] if jx == 1 else [])
            Yd(lambda e, jx=jx: e.dma_start(out=xldB[jx % 2][:, :],
                                            in_=x_d[jx * 128:(jx + 1) * 128, :]),
               reads=[], writes=[f"xld{jx % 2}"] + xal)
        if i < NTC:
            b = i // 4
            for h in range(H):
                col = head_col(h)
                base = (h % 2) * 64
                T_(lambda t, i=i, h=h, col=col, base=base, b=b: t.matmul(
                    psN[:, col:col + 65],
                    q_fm[base:base + 64, h // 2, i * 128:(i + 1) * 128],
                    kv_aug[base:base + 64, b * 1040 + h * 65:b * 1040 + (h + 1) * 65],
                    start=True, stop=True),
                   reads=[f"qf{h // 2}_{i // 8}", "kva_lo", "kva_hi"],
                   writes=[f"psN{h // 7}"])
        j = i - 2
        if 0 <= j < NTC:
            for kc in range(8):
                T_(lambda t, j=j, kc=kc: t.transpose(
                    psT[:, kc * 128:(kc + 1) * 128],
                    otmB[j % 2][:, kc * 128:(kc + 1) * 128], ident[:, :]),
                   reads=[f"otm{j % 2}", "ident"], writes=["psT"])
            eal = ["ofm2"] if j == 1 else []
            V_(lambda v, j=j: v.tensor_copy(out=ofmB[j % 2][:, :], in_=psT[:, :]),
               reads=["psT"], writes=[f"ofm{j % 2}"] + eal)
        if i < NTC:
            for jj, nh in [(0, 7), (1, 7), (2, 2)]:
                V_(lambda v, jj=jj, nh=nh: v.tensor_copy(
                    out=den[:, 7 * jj:7 * jj + nh],
                    in_=psN[:, jj * 512:jj * 512 + nh * 65].rearrange(
                        "p (h e) -> p h e", h=nh)[:, :, 64:65].squeeze(2)),
                   reads=[f"psN{jj}"], writes=["den"])
            V_(lambda v: v.tensor_scalar_max(out=den[:, :], in0=den[:, :],
                                             scalar1=1e-6),
               reads=["den"], writes=["den"])
            V_(lambda v: v.reciprocal(out=rr[:, :], in_=den[:, :]),
               reads=["den"], writes=["rr"])
            for jj, nh in [(0, 7), (1, 7), (2, 2)]:
                V_(lambda v, i=i, jj=jj, nh=nh: v.tensor_tensor(
                    out=otmB[i % 2][:, jj * 448:jj * 448 + nh * 64].rearrange(
                        "p (h e) -> p h e", h=nh),
                    in0=psN[:, jj * 512:jj * 512 + nh * 65].rearrange(
                        "p (h e) -> p h e", h=nh)[:, :, 0:64],
                    in1=rr[:, 7 * jj:7 * jj + nh].unsqueeze(2).broadcast_to(
                        [128, nh, 64]),
                    op=ALU.mult),
                   reads=[f"psN{jj}", "rr"], writes=[f"otm{i % 2}"])
        k = i - 3
        if 0 <= k < NTC:
            ps, pks = (psA, ["psA0", "psA1"]) if k % 2 == 0 else (psB, ["psB0", "psB1"])
            for half in range(2):
                sl = slice(half * 512, (half + 1) * 512)
                for kc in range(8):
                    T_(lambda t, k=k, kc=kc, sl=sl, ps=ps: t.matmul(
                        ps[:, sl], ofmB[k % 2][:, kc * 128:(kc + 1) * 128],
                        wob[:, kc, sl], start=(kc == 0), stop=False),
                       reads=[f"ofm{k % 2}", f"wo{kc}"], writes=[pks[half]],
                       cont=(kc != 0))
                T_(lambda t, sl=sl, ps=ps: t.matmul(
                    ps[:, sl], ones[0:1, 0:128], b2rows[0:1, 0, sl],
                    start=False, stop=True),
                   reads=["ones", "bohi"], writes=[pks[half]], cont=True)
            stage_f_tail(k)
        m = i - 4
        if 0 <= m < NTC:
            stage_h(m)
        kc2 = i - 3
        if 0 <= kc2 <= NTC:
            stage_j(kc2)
        pump_comp(5)
        pump_dma(3, i)
        if i == 18:
            load_tc(0)
            load_tc(1)
        if i == 19:
            while dq_state["nc"] < len(dq_state["jobs"]):
                pump_dma(99)
                pump_comp(99)
            emit_h_tc(0)
            load_tc(2)
            emit_h_tc(1)
        if i == 20:
            load_tc(3)
            emit_h_tc(2)
            emit_h_tc(3)

    while dq_state["nc"] < len(dq_state["jobs"]):
        pump_dma(2)
        pump_comp(2)

    def emit_fc2_tail(s8):
        tcs = [s8 * 2, s8 * 2 + 1]
        for i, tc in enumerate(tcs):
            sl5 = tc % 5
            q = (s8 % 2) * 2 + i
            for half in range(2):
                nk = f"psN{half}"
                sl = slice(half * 512, (half + 1) * 512)
                for fc in range(32):
                    T_(lambda t, fc=fc, sl=sl, q=q: t.matmul(
                        psN[:, sl], y_fm[:, fc, q * 128:(q + 1) * 128],
                        w2b[:, fc, sl], start=(fc == 0), stop=False),
                       reads=[f"yf{fc}", f"w2b{fc}"], writes=[nk],
                       cont=(fc != 0))
                T_(lambda t, sl=sl: t.matmul(
                    psN[:, sl], ones[0:1, 0:128], b2rows[0:1, 2, sl],
                    start=False, stop=True),
                   reads=["ones", "b2hi"], writes=[nk], cont=True)
            V_(lambda v, sl5=sl5: v.scalar_tensor_tensor(
                out=x1c[:, sl5, 0:512], in0=psN[:, 0:512], scalar=0.25,
                in1=x1c[:, sl5, 0:512], op0=ALU.mult, op1=ALU.add),
               reads=["psN0", f"x1c{sl5}"], writes=[f"x1c{sl5}"])
            V_(lambda v, sl5=sl5: v.scalar_tensor_tensor(
                out=x1c[:, sl5, 512:1024], in0=psN[:, 512:1024], scalar=0.25,
                in1=x1c[:, sl5, 512:1024], op0=ALU.mult, op1=ALU.add),
               reads=["psN1", f"x1c{sl5}"], writes=[f"x1c{sl5}"])
            V_(lambda v, i=i, sl5=sl5: v.reduce_sum(
                out=vb[:, i:i + 1], in_=x1c[:, sl5, :], axis=AXX),
               reads=[f"x1c{sl5}"], writes=[f"vb{i}"])
            S_(lambda sc, i=i, sl5=sl5: sc.activation(
                out=wstD[:, :], in_=x1c[:, sl5, :], func=AF.Square,
                accum_out=vb[:, 8 + i:9 + i]),
               reads=[f"x1c{sl5}"], writes=["wstD", f"vbs{i}"])
        for i in range(2):
            V_(lambda v, i=i: v.tensor_scalar_mul(
                out=vb[:, i:i + 1], in0=vb[:, i:i + 1], scalar1=1.0 / C),
               reads=[f"vb{i}"], writes=[f"vb{i}"])
            V_(lambda v, i=i: v.tensor_scalar_mul(
                out=vb[:, 8 + i:9 + i], in0=vb[:, 8 + i:9 + i], scalar1=1.0 / C),
               reads=[f"vbs{i}"], writes=[f"vbs{i}"])
            V_(lambda v, i=i: v.tensor_tensor(out=ss[:, 14:15],
                                              in0=vb[:, i:i + 1],
                                              in1=vb[:, i:i + 1], op=ALU.mult),
               reads=[f"vb{i}"], writes=["ss14"])
            V_(lambda v, i=i: v.tensor_tensor(out=vb[:, 8 + i:9 + i],
                                              in0=vb[:, 8 + i:9 + i],
                                              in1=ss[:, 14:15], op=ALU.subtract),
               reads=[f"vbs{i}", "ss14"], writes=[f"vbs{i}"])
        S_(lambda sc: sc.activation(out=vb[:, 12:14], in_=vb[:, 8:10],
                                    func=AF.Sqrt, bias=epsc[:, 1:2]),
           reads=["vbs0", "vbs1", "epsc"], writes=["sd4"])
        V_(lambda v: v.reciprocal(out=vb[:, 12:14], in_=vb[:, 12:14]),
           reads=["sd4"], writes=["sd4"])
        for i, tc in enumerate(tcs):
            sl5 = tc % 5
            V_(lambda v, i=i, sl5=sl5: v.tensor_scalar(
                out=x1c[:, sl5, :], in0=x1c[:, sl5, :],
                scalar1=vb[:, i:i + 1], scalar2=vb[:, 12 + i:13 + i],
                op0=ALU.subtract, op1=ALU.mult),
               reads=[f"x1c{sl5}", f"vb{i}", "sd4"], writes=[f"x1c{sl5}"])
            Yd(lambda e, tc=tc, sl5=sl5: e.dma_start(
                out=y_d[tc * 128:(tc + 1) * 128, :], in_=x1c[:, sl5, :]),
               reads=[f"x1c{sl5}"], writes=[f"yo{tc}"])

    for s4 in range(4):
        for nch in range(32):
            qsel = nch % 4
            pst = psA if qsel < 2 else psB
            bank = pst[:, (qsel % 2) * 512:(qsel % 2 + 1) * 512]
            bkey = ["psA0", "psA1", "psB0", "psB1"][qsel]
            nxt = ([["psA0", "psA1", "psB0", "psB1"][(qsel + 1) % 4]]
                   if nch % 2 == 0 else [])
            for kc in range(8):
                T_(lambda t, bank=bank, nch=nch, kc=kc: t.matmul(
                    bank, w1b[:, kc, nch * 128:(nch + 1) * 128],
                    h_fm[:, kc, :], start=(kc == 0), stop=(kc == 7)),
                   reads=([f"w1b{kc}"] + [f"h{kc}_{qq}" for qq in range(4)]
                          + (nxt if kc == 0 else [])),
                   writes=[bkey], cont=(kc != 0))
            S_(lambda sc, bank=bank, nch=nch: sc.activation(
                out=y_fm[:, nch, :], in_=bank, func=AF.Silu,
                bias=b1f[:, nch:nch + 1]),
               reads=[bkey, "b1f"],
               writes=[f"yf{nch}"] + (Y0AL if (s4 == 0 and nch == 0) else []))
        load_tc(4 * s4 + 4)
        emit_h_tc(4 * s4 + 4)
        emit_fc2_tail(2 * s4)
        load_tc(4 * s4 + 5)
        emit_h_tc(4 * s4 + 5)
        load_tc(4 * s4 + 6)
        emit_h_tc(4 * s4 + 6)
        emit_fc2_tail(2 * s4 + 1)
        load_tc(4 * s4 + 7)
        emit_h_tc(4 * s4 + 7)

    r.add("Y", lambda e: e.nop(), reads=[f"yo{tc}" for tc in range(NTC)],
          writes=[])

    ops = r.resolve()
    streams = {"T": [], "V": [], "S": [], "P": [], "Y": []}
    for op in ops:
        streams[op.eng].append(op)

    measured = {}

    def run_stream(eng_obj, lst):
        hw = {}
        for op in lst:
            for sem_key, val in op.waits:
                if hw.get(sem_key, 0) >= val:
                    continue
                eng_obj.wait_ge(sems[sem_key], val)
                hw[sem_key] = val
            if op.issue_wait is not None:
                skey, val = op.issue_wait
                if hw.get(skey, 0) < val:
                    eng_obj.wait_ge(sems[skey], val)
                    hw[skey] = val
            n0 = nc.n_instructions()
            ins = op.fn(eng_obj)
            if op.queue:
                measured[op.idx] = nc.n_instructions() - n0
            if op.inc:
                ins.then_inc(sems[op.qsem if op.queue else op.eng],
                             16 if op.queue else 1)

    with nc.Block() as blk:
        @blk.tensor
        def _(e):
            run_stream(e, streams["T"])

        @blk.vector
        def _(e):
            run_stream(e, streams["V"])

        @blk.scalar
        def _(e):
            run_stream(e, streams["S"])

        @blk.gpsimd
        def _(e):
            run_stream(e, streams["P"])

        @blk.sync
        def _(e):
            run_stream(e, streams["Y"])

    if dma_counts is None:
        return measured
    return nc


_NC_CACHE = [None]


def kernel(**inputs):
    inp = {k: np.asarray(v) for k, v in inputs.items()}
    x = inp["L0_attn_x"].astype(np.float32, copy=False)

    wkT = np.ascontiguousarray(inp["L0_attn_Wk"].T.astype(np.float32, copy=False))
    wvT = np.ascontiguousarray(inp["L0_attn_Wv"].T.astype(np.float32, copy=False))
    wqT = np.ascontiguousarray(inp["L0_attn_Wq"].T.astype(np.float32, copy=False))
    woT = np.ascontiguousarray(inp["L0_attn_Wo"].T.astype(np.float32, copy=False))
    brows = np.ascontiguousarray(np.stack([
        inp["L0_attn_bk"], inp["L0_attn_bv"], inp["L0_attn_bq"],
        inp["L0_attn_bo"], inp["L0_ffn_fc2_bias"]]).astype(np.float32))
    gamf = np.ascontiguousarray(
        inp["L0_ffn_ln_gamma"].astype(np.float32).reshape(8, 128).T)
    betf = np.ascontiguousarray(
        inp["L0_ffn_ln_beta"].astype(np.float32).reshape(8, 128).T)
    b1f = np.ascontiguousarray(
        inp["L0_ffn_fc1_bias"].astype(np.float32).reshape(32, 128).T)
    bqf = np.ascontiguousarray(
        inp["L0_attn_bq"].astype(np.float32).reshape(8, 128).T)
    q1 = np.ascontiguousarray(
        inp["L0_ffn_fc1_q"].astype(np.uint8, copy=False)
        .reshape(8, 128, 8, 512).transpose(0, 2, 1, 3).reshape(64 * 128, 512))
    s1 = np.ascontiguousarray(inp["L0_ffn_fc1_scales"].astype(np.float32, copy=False))
    q2 = np.ascontiguousarray(
        inp["L0_ffn_fc2_q"].astype(np.uint8, copy=False)
        .reshape(32, 128, 2, 512).transpose(0, 2, 1, 3).reshape(64 * 128, 512))
    s2 = np.ascontiguousarray(inp["L0_ffn_fc2_scales"].astype(np.float32, copy=False))

    if _NC_CACHE[0] is None:
        _NC_CACHE[0] = build_nc()
    nc = _NC_CACHE[0]

    in_maps = []
    for c in range(N_CORES):
        xc = np.ascontiguousarray(x[:, c * 512:(c + 1) * 512, :].reshape(M, C))
        in_maps.append(dict(
            x=xc, xT=np.ascontiguousarray(xc.T),
            wkT=wkT, wvT=wvT, wqT=wqT, woT=woT, brows=brows,
            gamf=gamf, betf=betf, b1f=b1f, bqf=bqf,
            q1=q1, s1=s1, q2=q2, s2=s2))

    res = run_bass_kernel_spmd(nc, in_maps, core_ids=list(range(N_CORES)))
    out = np.empty((B, T, C), np.float32)
    for c in range(N_CORES):
        out[:, c * 512:(c + 1) * 512, :] = res.results[c]["y"].reshape(B, 512, C)
    return out


# version 26
# speedup vs baseline: 1.5016x; 1.5016x over previous
import numpy as np
import concourse.bass as bass
import concourse.mybir as mybir
from concourse.bass_utils import run_bass_kernel_spmd

F32 = mybir.dt.float32
BF16 = mybir.dt.bfloat16
I32 = mybir.dt.int32
U8 = mybir.dt.uint8
AF = mybir.ActivationFunctionType
ALU = mybir.AluOpType
AXX = mybir.AxisListType.X

KB = 1024
N_CORES = 8
B, T, C = 4, 4096, 1024
H = 16
M = 2048
NTC = 16
F = 4096
EPS = 1e-5


def head_col(h):
    return (h // 7) * 512 + (h % 7) * 65


QROT = 8
SAFE_EDGES = True


class Op:
    __slots__ = ("eng", "fn", "reads", "writes", "queue", "idx", "inc",
                 "inc_val", "waits", "deps", "qsem", "issue_wait", "cont",
                 "guar")

    def __init__(self, eng, fn, reads, writes, queue, cont=False):
        self.eng = eng
        self.fn = fn
        self.reads = tuple(reads)
        self.writes = tuple(writes)
        self.queue = queue
        self.inc = False
        self.inc_val = None
        self.waits = []
        self.deps = []
        self.qsem = None
        self.issue_wait = None
        self.cont = cont
        self.guar = {}


class Rec:
    def __init__(self, dma_counts=None):
        self.ops = []
        self.dma_counts = dma_counts

    def add(self, eng, fn, reads=(), writes=(), dma=False, cont=False):
        self.ops.append(Op(eng, fn, reads, writes,
                           eng + "q" if dma else None, cont))

    def resolve(self):
        ops = self.ops
        for i, op in enumerate(ops):
            op.idx = i
        quse = {}
        scount = {}
        for op in ops:
            if op.queue is not None:
                op.inc = True
                i = quse.get(op.queue, 0)
                quse[op.queue] = i + 1
                skey = f"{op.queue}{i % QROT}"
                op.qsem = skey
                prev = scount.get(skey, 0)
                op.issue_wait = (skey, prev) if prev else None
                k = self.dma_counts.get(op.idx, 1) if self.dma_counts else 1
                scount[skey] = prev + 16 * k
                op.inc_val = scount[skey]
        writer = {}
        readers = {}
        for op in ops:
            deps = {}

            def dep_on(d, op=op, deps=deps):
                if d is None or d is op:
                    return
                src = d.qsem if d.queue else d.eng
                if d.queue is None and op.queue is not None and d.eng == op.eng:
                    return
                if (d.queue is None and op.queue is None
                        and d.eng == "T" and op.eng == "T" and op.cont):
                    return
                if (not SAFE_EDGES and d.queue is None and op.queue is None
                        and d.eng == op.eng and d.eng in ("V", "S")):
                    return
                cur = deps.get(src)
                if cur is None or d.idx > cur.idx:
                    deps[src] = d

            for b_ in op.reads:
                dep_on(writer.get(b_))
            for b_ in op.writes:
                dep_on(writer.get(b_))
                for rd in readers.get(b_, {}).values():
                    dep_on(rd)
            op.deps = list(deps.values())
            for d in op.deps:
                d.inc = True
            for b_ in op.reads:
                readers.setdefault(b_, {})[op.queue or op.eng] = op
            for b_ in op.writes:
                writer[b_] = op
                readers[b_] = {}
        count = {}
        for op in ops:
            if op.queue is None and op.inc:
                key = op.eng
                count[key] = count.get(key, 0) + 1
                op.inc_val = count[key]
        stream_cur = {}
        for op in ops:
            me = op.eng
            cur = stream_cur.setdefault(me, {})
            for d in sorted(op.deps, key=lambda d: -d.idx):
                key = d.qsem if d.queue else d.eng
                if cur.get(key, 0) < d.inc_val:
                    op.waits.append((key, d.inc_val))
                    cur[key] = d.inc_val
                for k2, v2 in d.guar.items():
                    if cur.get(k2, 0) < v2:
                        cur[k2] = v2
            g = dict(cur)
            if op.inc:
                k = op.qsem if op.queue else op.eng
                g[k] = max(g.get(k, 0), op.inc_val)
            op.guar = g
        return ops


def build_nc():
    counts = _build_inner(None)
    return _build_inner(counts)


def _build_inner(dma_counts):
    nc = bass.Bass("TRN2", target_bir_lowering=False, debug=False)

    x_d = nc.dram_tensor("x", [M, C], F32, kind="ExternalInput")
    xT_d = nc.dram_tensor("xT", [C, M], F32, kind="ExternalInput")
    wkT_d = nc.dram_tensor("wkT", [C, C], F32, kind="ExternalInput")
    wvT_d = nc.dram_tensor("wvT", [C, C], F32, kind="ExternalInput")
    wqT_d = nc.dram_tensor("wqT", [C, C], F32, kind="ExternalInput")
    woT_d = nc.dram_tensor("woT", [C, C], F32, kind="ExternalInput")
    brows_d = nc.dram_tensor("brows", [5, C], F32, kind="ExternalInput")
    gamf_d = nc.dram_tensor("gamf", [128, 8], F32, kind="ExternalInput")
    betf_d = nc.dram_tensor("betf", [128, 8], F32, kind="ExternalInput")
    b1f_d = nc.dram_tensor("b1f", [128, 32], F32, kind="ExternalInput")
    bqf_d = nc.dram_tensor("bqf", [128, 8], F32, kind="ExternalInput")
    q1_d = nc.dram_tensor("q1", [64 * 128, 512], U8, kind="ExternalInput")
    s1_d = nc.dram_tensor("s1", [8, F], F32, kind="ExternalInput")
    q2_d = nc.dram_tensor("q2", [64 * 128, 512], U8, kind="ExternalInput")
    s2_d = nc.dram_tensor("s2", [32, C], F32, kind="ExternalInput")
    y_d = nc.dram_tensor("y", [M, C], F32, kind="ExternalOutput")

    x1_d = nc.dram_tensor("x1spill", [M, C], F32)
    cc_in = nc.dram_tensor("cc_in", [64, 4160], F32)
    cc_out = nc.dram_tensor("cc_out", [64, 4160], F32)

    A = nc.alloc_sbuf_tensor_at
    O = 16 * KB
    ident = A("ident", [128, 128], BF16, offset=O + 1024)
    ones = A("ones", [1, 512], BF16, offset=O + 1280)
    gamf = A("gamf_s", [128, 8], F32, offset=O + 2304)
    betf = A("betf_s", [128, 8], F32, offset=O + 2336)
    b1f = A("b1f_s", [128, 32], F32, offset=O + 2368)
    stat = A("stat", [128, 64], F32, offset=O + 2496)
    bqf = A("bqf_s", [128, 8], F32, offset=O + 3072)
    ss = A("ss", [128, 16], F32, offset=O + 2752)
    vb = A("vb", [128, 16], F32, offset=O + 2816)
    epsc = A("epsc", [128, 4], F32, offset=O + 2880)
    den = A("den", [128, 16], F32, offset=O + 2944)
    rr = A("rr", [128, 16], F32, offset=O + 3008)
    b2rows = A("b2rows", [1, 4, 1024], BF16, offset=20 * KB)
    b1rows = A("b1rows", [1, 6, 1024], BF16, offset=28 * KB)
    qst = A("qst", [128, 4, 512], U8, offset=28 * KB)
    sbc = A("sbc", [128, 4, 512], F32, offset=30 * KB)
    u_t2 = A("u_t2", [128, 1024], F32, offset=152 * KB)
    xnt = A("xnt", [128, 1024], BF16, offset=38 * KB)
    wstC = A("wstC", [128, 1024], BF16, offset=129 * KB)
    wstD = A("wstD", [128, 1024], BF16, offset=156 * KB)
    xld2 = A("xld2", [128, 1024], F32, offset=131 * KB)
    ofm2 = A("ofm2", [128, 1024], BF16, offset=22 * KB)
    otm2 = A("otm2", [128, 1024], BF16, offset=26 * KB)
    t1q = A("t1q", [128, 1024], F32, offset=72 * KB)
    t2q = A("t2q", [128, 1024], F32, offset=76 * KB)
    stgB = A("stgB", [128, 2048], F32, offset=48 * KB)
    x1c = A("x1c", [128, 5, 1024], F32, offset=136 * KB)
    stgA = A("stgA", [128, 2048], F32, offset=40 * KB)
    stgC = A("stgC", [128, 2048], F32, offset=72 * KB)
    stgD = A("stgD", [128, 2048], F32, offset=80 * KB)
    wsg = A("wsg", [128, 4, 1024], F32, offset=159 * KB)
    brf = A("brf", [1, 1024], F32, offset=100 * KB)
    t1 = A("t1", [128, 1024], F32, offset=48 * KB)
    t2 = A("t2", [128, 1024], F32, offset=52 * KB)
    wkb = A("wkb", [128, 8, 1024], BF16, offset=56 * KB)
    wqb = A("wqb", [128, 8, 1024], BF16, offset=56 * KB)
    ktiles = A("ktiles", [128, 4, 1024], BF16, offset=72 * KB)
    v_aug = A("v_aug", [128, 4, 1040], BF16, offset=80 * KB)
    kv_stage = A("kv_stage", [64, 2, 1040], F32, offset=88 * KB + 256)
    w1b = A("w1b", [128, 8, 4096], BF16, offset=40 * KB)
    x_fm = A("x_fm", [128, 8, 2048], BF16, offset=104 * KB)
    kvr = A("kvr", [64, 4160], F32, offset=104 * KB)
    xld = A("xld", [128, 1024], F32, offset=104 * KB)
    ofm = A("ofm", [128, 1024], BF16, offset=108 * KB)
    otm = A("otm", [128, 1024], BF16, offset=110 * KB)
    u_t = A("u_t", [128, 1024], F32, offset=112 * KB)
    x1t = A("x1t", [128, 1024], F32, offset=116 * KB)
    kv_aug = A("kv_aug", [128, 4160], BF16, offset=120 * KB + 512)
    y_fm = A("y_fm", [128, 32, 512], BF16, offset=104 * KB)
    h_fm = A("h_fm", [128, 8, 512], BF16, offset=28 * KB)
    wvb = A("wvb", [128, 8, 1024], BF16, offset=136 * KB)
    wob = A("wob", [128, 8, 1024], BF16, offset=136 * KB)
    w2b = A("w2b", [128, 32, 1024], BF16, offset=159 * KB)
    q_fm = A("q_fm", [128, 8, 2048], BF16, offset=191 * KB)
    bk_bc = A("bk_bc", [128, 1024], F32, offset=191 * KB)
    bv_bc = A("bv_bc", [128, 1024], F32, offset=195 * KB)
    t0p = A("t0p", [128, 1024], F32, offset=199 * KB)
    t0q = A("t0q", [128, 1024], F32, offset=48 * KB)

    psA = nc.alloc_psum_tensor("psA", [128, 1024], F32)
    psB = nc.alloc_psum_tensor("psB", [128, 1024], F32)
    psN = nc.alloc_psum_tensor("psN", [128, 1536], F32)
    psT = nc.alloc_psum_tensor("psT", [128, 1024], BF16)

    sem_keys = ["T", "V", "S", "P", "Y"]
    for q in ["Yq", "Sq", "Pq"]:
        sem_keys += [f"{q}{i}" for i in range(QROT)]
    sems = {k: nc.alloc_semaphore("sem_" + k) for k in sem_keys}

    r = Rec(dma_counts)
    T_ = lambda fn, reads=(), writes=(), cont=False: r.add("T", fn, reads,
                                                           writes, cont=cont)
    V_ = lambda fn, reads=(), writes=(): r.add("V", fn, reads, writes)
    S_ = lambda fn, reads=(), writes=(): r.add("S", fn, reads, writes)
    P_ = lambda fn, reads=(), writes=(): r.add("P", fn, reads, writes)
    Yd = lambda fn, reads=(), writes=(): r.add("Y", fn, reads, writes, dma=True)
    Sd = lambda fn, reads=(), writes=(): r.add("S", fn, reads, writes, dma=True)
    Pd = lambda fn, reads=(), writes=(): r.add("P", fn, reads, writes, dma=True)

    XFM = [f"xfm{c}" for c in range(8)]
    VAK = [f"va{i}" for i in range(4)]
    KVS = ["kvs0", "kvs1"]
    WOK = [f"wo{c}" for c in range(8)]

    P_(lambda g: g.memset(ident[:, :], 0.0), writes=["ident", "ptok"])
    P_(lambda g: g.affine_select(out=ident[:, :], in_=ident[:, :],
                                 compare_op=ALU.not_equal, fill=1.0, base=0,
                                 pattern=[[-1, 128]], channel_multiplier=1),
       reads=["ident", "ptok"], writes=["ident", "ptok"])
    P_(lambda g: g.memset(ones[:, :], 1.0), writes=["ones", "ptok"])
    P_(lambda g: g.memset(epsc[:, 0:1], EPS), writes=["epsc", "ptok"])
    P_(lambda g: g.memset(epsc[:, 1:2], EPS / 4.0), writes=["epsc", "ptok"])
    P_(lambda g: g.memset(epsc[:, 2:3], -128.0), writes=["epsc", "ptok"])
    P_(lambda g: g.memset(stat[:, 32:64], 1024.0),
       writes=["ptok"] + [f"sa{p}{c}" for p in range(2) for c in "0123"])

    XSTG = [stgA, stgB, stgC, stgD]
    for c in range(8):
        stg, skey = XSTG[c % 4], f"xsg{c % 4}"
        Yd(lambda e, c=c, stg=stg: e.dma_start(
            out=stg[:, :], in_=xT_d[c * 128:(c + 1) * 128, :]),
           reads=[], writes=[skey])
        V_(lambda v, c=c, stg=stg: v.tensor_copy(out=x_fm[:, c, :],
                                                 in_=stg[:, :]),
           reads=[skey], writes=[f"xfm{c}"])
    def load_w(dram, dst, dstkey, alias=None):
        for c in range(8):
            slot = c % 4
            Pd(lambda e, c=c, slot=slot, dram=dram: e.dma_start(
                out=wsg[:, slot, :], in_=dram[c * 128:(c + 1) * 128, :]),
               writes=[f"wsg{slot}"])
            al = [f"{alias}{c}"] if alias else []
            S_(lambda sc, c=c, slot=slot, dst=dst: sc.copy(
                out=dst[:, c, :], in_=wsg[:, slot, :]),
               reads=[f"wsg{slot}"], writes=[f"{dstkey}{c}"] + al)

    load_w(wkT_d, wkb, "wk")
    load_w(wvT_d, wvb, "wv")
    P_(lambda g: g.memset(v_aug[:, :, :], 1.0),
       writes=VAK + ["ptok", "xsg3"])

    Yd(lambda e: e.dma_start(out=gamf[:, :], in_=gamf_d[:, :]), writes=["gamf"])
    Yd(lambda e: e.dma_start(out=betf[:, :], in_=betf_d[:, :]), writes=["betf"])
    Yd(lambda e: e.dma_start(out=b1f[:, :], in_=b1f_d[:, :]), writes=["b1f"])
    Yd(lambda e: e.dma_start(out=bqf[:, :], in_=bqf_d[:, :]), writes=["bqf"])

    for row, tgt, slot, nm in [(0, b1rows, 0, "bk"), (1, b1rows, 2, "bv"),
                               (3, b2rows, 0, "bo"),
                               (4, b2rows, 2, "b2")]:
        hi = nm + "hi"
        Yd(lambda e, row=row: e.dma_start(out=brf[:, :],
                                          in_=brows_d[row:row + 1, :]),
           writes=["brf"])
        S_(lambda sc, tgt=tgt, slot=slot: sc.copy(
            out=tgt[:, slot, :], in_=brf[:, :]),
           reads=["brf"], writes=[hi])

    for half in range(2):
        sl = slice(half * 512, (half + 1) * 512)
        T_(lambda t, sl=sl: t.matmul(psA[:, sl], ones[0:1, 0:128],
                                     b1rows[0:1, 0, sl], start=True, stop=True),
           reads=["ones", "bkhi"], writes=["psA0" if half == 0 else "psA1"])
        T_(lambda t, sl=sl: t.matmul(psB[:, sl], ones[0:1, 0:128],
                                     b1rows[0:1, 2, sl], start=True, stop=True),
           reads=["ones", "bvhi"], writes=["psB0" if half == 0 else "psB1"])
    V_(lambda v: v.tensor_copy(out=bk_bc[:, :], in_=psA[:, :]),
       reads=["psA0", "psA1"], writes=["bkbc"])
    V_(lambda v: v.tensor_copy(out=bv_bc[:, :], in_=psB[:, :]),
       reads=["psB0", "psB1"], writes=["bvbc"])

    for b in range(B):
        for i in range(4):
            tc = 4 * b + i
            for half in range(2):
                pk = f"psA{half}"
                sl = slice(half * 512, (half + 1) * 512)
                for kc in range(8):
                    T_(lambda t, kc=kc, sl=sl, tc=tc: t.matmul(
                        psA[:, sl], x_fm[:, kc, tc * 128:(tc + 1) * 128],
                        wkb[:, kc, sl], start=(kc == 0), stop=(kc == 7)),
                       reads=[f"xfm{kc}", f"wk{kc}"], writes=[pk],
                       cont=(kc != 0))
            kal = ["xsg1"] if tc == 0 else []
            V_(lambda v: v.tensor_tensor(out=t0p[:, :], in0=psA[:, :],
                                         in1=bk_bc[:, :], op=ALU.add),
               reads=["psA0", "psA1", "bkbc"], writes=["t0"])
            S_(lambda sc: sc.activation(out=t1[:, :], in_=t0p[:, :],
                                        func=AF.Relu, scale=-1.0),
               reads=["t0"], writes=["t1"] + kal)
            S_(lambda sc: sc.activation(out=t2[:, :], in_=t1[:, :],
                                        func=AF.Exp, scale=-1.0),
               reads=["t1"], writes=["t2"] + kal)
            V_(lambda v, i=i: v.scalar_tensor_tensor(
                out=ktiles[:, i, :], in0=t0p[:, :], scalar=0.0,
                in1=t2[:, :], op0=ALU.max, op1=ALU.add),
               reads=["t0", "t2"],
               writes=[f"kt{i}"] + (["xsg2"] if tc < 4 else []))
            for half in range(2):
                pk = f"psB{half}"
                sl = slice(half * 512, (half + 1) * 512)
                for kc in range(8):
                    T_(lambda t, kc=kc, sl=sl, tc=tc: t.matmul(
                        psB[:, sl], x_fm[:, kc, tc * 128:(tc + 1) * 128],
                        wvb[:, kc, sl], start=(kc == 0), stop=(kc == 7)),
                       reads=[f"xfm{kc}", f"wv{kc}"], writes=[pk],
                       cont=(kc != 0))
            V_(lambda v: v.tensor_tensor(out=t0p[:, :], in0=psB[:, :],
                                         in1=bv_bc[:, :], op=ALU.add),
               reads=["psB0", "psB1", "bvbc"], writes=["t0"])
            S_(lambda sc, i=i: sc.activation(
                out=v_aug[:, i, :].rearrange("p (h e) -> p h e", h=16)[:, :, 0:64],
                in_=t0p[:, :].rearrange("p (h e) -> p h e", h=16),
                func=AF.Identity),
               reads=["t0"], writes=[f"va{i}"])
        for h in range(H):
            col = head_col(h)
            for i in range(4):
                T_(lambda t, h=h, i=i, col=col: t.matmul(
                    psN[0:64, col:col + 65],
                    ktiles[:, i, h * 64:(h + 1) * 64],
                    v_aug[:, i, h * 65:(h + 1) * 65],
                    start=(i == 0), stop=(i == 3)),
                   reads=[f"kt{i}", f"va{i}"], writes=[f"psN{h // 7}"],
                   cont=(i != 0))
        buf = b % 2
        for j, nh in [(0, 7), (1, 7), (2, 2)]:
            V_(lambda v, buf=buf, j=j, nh=nh: v.tensor_copy(
                out=kv_stage[:, buf, :].rearrange(
                    "p (h e) -> p h e", h=16)[:, 7 * j:7 * j + nh, :],
                in_=psN[0:64, j * 512:j * 512 + nh * 65].rearrange(
                    "p (h e) -> p h e", h=nh)),
               reads=[f"psN{j}"], writes=[f"kvs{buf}"])
        Yd(lambda e, b=b, buf=buf: e.dma_start(
            out=cc_in[:, b * 1040:(b + 1) * 1040],
            in_=kv_stage[:, buf, :]),
           reads=[f"kvs{buf}"], writes=[f"ccin{b}"])

    KTK = [f"kt{i}" for i in range(4)]
    W1AL = {0: ["xsg0"], 1: ["t1", "t2", "xsg1", "t0q"],
            2: ["wq0", "wq1", "wq2", "wq3"], 3: ["wq4", "wq5", "wq6", "wq7"],
            4: KTK + ["t1q", "t2q", "xsg2"], 5: VAK + ["xsg3"],
            6: VAK + KVS, 7: KVS}

    def dq_aps(job):
        kind, ch, s = job
        if kind == "w1":
            qsrc = q1_d[(ch * 8 + s) * 128:(ch * 8 + s + 1) * 128, :]
            ssrc = s1_d[ch:ch + 1, s * 512:(s + 1) * 512]
            dst = w1b[:, ch, s * 512:(s + 1) * 512]
            dkey = f"w1b{ch}"
            al = W1AL.get(ch, []) if s == 0 else []
        else:
            qsrc = q2_d[(ch * 2 + s) * 128:(ch * 2 + s + 1) * 128, :]
            ssrc = s2_d[ch:ch + 1, s * 512:(s + 1) * 512]
            dst = w2b[:, ch, s * 512:(s + 1) * 512]
            dkey = f"w2b{ch}"
            al = ([f"qf{(ch - 16) // 2}_{(ch - 16) % 2}"]
                  if (ch >= 16 and s == 0) else
                  [f"wsg{ch // 2}"] if (ch < 8 and s == 0) else [])
        return qsrc, ssrc, dst, dkey, al

    def emit_dq_dma(job, slot, first=False):
        qsrc, ssrc, dst, dkey, al = dq_aps(job)
        qk, sk = f"qst{slot}", f"sbc{slot}"
        qal = ["bkhi"] if first else []
        sal = ["bvhi"] if first else []
        Yd(lambda e, qsrc=qsrc, slot=slot: e.dma_start(
            out=qst[:, slot, :], in_=qsrc), writes=[qk] + qal)
        Yd(lambda e, ssrc=ssrc, slot=slot: e.dma_start(
            out=sbc[:, slot, :], in_=ssrc.partition_broadcast(128)),
           writes=[sk] + sal)

    def emit_dq_comp(job, slot):
        qsrc, ssrc, dst, dkey, al = dq_aps(job)
        qk, sk = f"qst{slot}", f"sbc{slot}"
        V_(lambda v, dst=dst, slot=slot: v.scalar_tensor_tensor(
            out=dst, in0=qst[:, slot, :], scalar=128.0,
            in1=sbc[:, slot, :], op0=ALU.subtract, op1=ALU.mult),
           reads=[qk, sk], writes=[dkey] + al)

    dq_p2 = ([("w1", kc, s) for kc in (5, 6, 7, 0) for s in range(8)] +
             [("w2", fc, s) for fc in range(8, 16) for s in range(2)])
    dq_c = ([("w1", kc, s) for kc in (1, 2, 3, 4) for s in range(8)] +
            [("w2", fc, s) for fc in range(8) for s in range(2)] +
            [("w2", fc, s) for fc in range(16, 32, 2) for s in range(2)] +
            [("w2", fc, s) for fc in range(17, 32, 2) for s in range(2)])
    dq_gates = ([0] * 48 + [9] * 16 + [16] * 16)
    dq_state = {"nd": 0, "nc": 0, "tot0": 0, "jobs": dq_p2,
                "gates": [0] * len(dq_p2)}

    def dq_switch(jobs, gates=None):
        st = dq_state
        assert st["nd"] == st["nc"] == len(st["jobs"])
        st["tot0"] += len(st["jobs"])
        st["jobs"] = jobs
        st["gates"] = gates if gates is not None else [0] * len(jobs)
        st["nd"] = st["nc"] = 0

    def pump_dma(k, it=10 ** 9):
        st = dq_state
        while (k > 0 and st["nd"] < len(st["jobs"])
               and st["nd"] < st["nc"] + 4
               and st["gates"][st["nd"]] <= it):
            j = st["nd"]
            emit_dq_dma(st["jobs"][j], (st["tot0"] + j) % 4,
                        first=(st["tot0"] + j == 0))
            st["nd"] += 1
            k -= 1

    def pump_comp(k):
        st = dq_state
        while k > 0 and st["nc"] < min(st["nd"], len(st["jobs"])):
            j = st["nc"]
            emit_dq_comp(st["jobs"][j], (st["tot0"] + j) % 4)
            st["nc"] += 1
            k -= 1

    def pump(k):
        pump_dma(k)
        pump_comp(k)

    load_w(wqT_d, wqb, "wq", alias="wk")
    load_w(woT_d, wob, "wo", alias="wv")

    P_(lambda g: g.collective_compute(
        "AllReduce", ALU.add, replica_groups=[list(range(N_CORES))],
        ins=[cc_in[:, :]], outs=[cc_out[:, :]]),
       reads=[f"ccin{b_}" for b_ in range(B)] + ["ptok"],
       writes=["ccout", "ptok"])

    for nch in range(8):
        for strip in range(2):
            ps, pks = (psA, ["psA0", "psA1"]) if (nch * 2 + strip) % 2 == 0 \
                else (psB, ["psB0", "psB1"])
            for half in range(2):
                sl = slice(half * 512, (half + 1) * 512)
                t0 = strip * 1024 + half * 512
                for kc in range(8):
                    T_(lambda t, ps=ps, nch=nch, kc=kc, sl=sl, t0=t0: t.matmul(
                        ps[:, sl], wqb[:, kc, nch * 128:(nch + 1) * 128],
                        x_fm[:, kc, t0:t0 + 512],
                        start=(kc == 0), stop=(kc == 7)),
                       reads=[f"wq{kc}", f"xfm{kc}"], writes=[pks[half]],
                       cont=(kc != 0))
            qa1 = ["kt0", "kt1"] if (nch == 0 and strip == 0) else []
            qa2 = ["kt2", "kt3"] if (nch == 0 and strip == 0) else []
            qa0 = (["t1", "xsg1"] if (nch == 0 and strip == 0) else [])
            V_(lambda v, ps=ps, nch=nch: v.tensor_scalar_add(
                out=t0q[:, :], in0=ps[:, :], scalar1=bqf[:, nch:nch + 1]),
               reads=pks + ["bqf"], writes=["t0q"] + qa0)
            S_(lambda sc: sc.activation(out=t1q[:, :], in_=t0q[:, :],
                                        func=AF.Relu, scale=-1.0),
               reads=["t0q"], writes=["t1q"] + qa1)
            S_(lambda sc: sc.activation(out=t2q[:, :], in_=t1q[:, :],
                                        func=AF.Exp, scale=-1.0),
               reads=["t1q"], writes=["t2q"] + qa2)
            V_(lambda v, nch=nch, strip=strip: v.scalar_tensor_tensor(
                out=q_fm[:, nch, strip * 1024:(strip + 1) * 1024],
                in0=t0q[:, :], scalar=0.0, in1=t2q[:, :],
                op0=ALU.max, op1=ALU.add),
               reads=["t0q", "t2q"],
               writes=[f"qf{nch}_{strip}"] +
                      (["bkbc", "bvbc"] if (nch == 0 and strip == 0) else []) +
                      (["t0"] if (nch == 2 and strip == 0) else []))
            pump_dma(2)
            pump_comp(2)

    while dq_state["nc"] < len(dq_state["jobs"]):
        pump_dma(2)
        pump_comp(2)
    dq_switch(dq_c, dq_gates)

    Yd(lambda e: e.dma_start(out=kvr[:, :], in_=cc_out[:, :]),
       reads=["ccout"], writes=["kvr"] + XFM)
    V_(lambda v: v.tensor_copy(out=kv_aug[0:64, :], in_=kvr[:, :]),
       reads=["kvr"], writes=["kva_lo"])
    Pd(lambda e: e.dma_start(out=kv_aug[64:128, :], in_=kv_aug[0:64, :]),
       reads=["kva_lo"], writes=["kva_hi"])

    QSB = [f"qst{j}" for j in range(4)] + [f"sbc{j}" for j in range(3)]
    Y0AL = ["xld0", "ofm0", "otm0", "u0", "x1t", "kva_lo", "kva_hi",
            "wstC", "xld1"]
    X1AL = {0: ["wo0", "wo1"], 1: ["wo2", "wo3"], 2: ["wo4", "wo5"],
            3: ["wo6", "wo7"], 4: ["u1"]}

    def load_tc(tc):
        if not 0 <= tc < NTC:
            return
        sl5 = tc % 5
        x1al = X1AL.get(sl5, []) if tc < 5 else []
        Pd(lambda e, tc=tc, sl5=sl5: e.dma_start(
            out=x1c[:, sl5, :], in_=x1_d[tc * 128:(tc + 1) * 128, :]),
           reads=[f"x1d{tc}"], writes=[f"x1c{sl5}"] + x1al)

    def emit_h_tc(tc):
        if not 0 <= tc < NTC:
            return
        sl5 = tc % 5
        q = tc % 4
        pos = q * 128
        V_(lambda v, tc=tc, sl5=sl5: v.tensor_scalar(
            out=xnt[:, :], in0=x1c[:, sl5, :],
            scalar1=stat[:, 2 * tc:2 * tc + 1],
            scalar2=stat[:, 2 * tc + 1:2 * tc + 2],
            op0=ALU.subtract, op1=ALU.mult),
           reads=[f"x1c{sl5}", f"mr_{tc}"], writes=["xnt"])
        for kc in range(8):
            T_(lambda t, kc=kc: t.transpose(
                psT[:, kc * 128:(kc + 1) * 128],
                xnt[:, kc * 128:(kc + 1) * 128], ident[:, :]),
               reads=["xnt", "ident"], writes=["psT"])
        hal = QSB if tc == 0 else []
        for kc in range(8):
            V_(lambda v, kc=kc, pos=pos, q=q: v.tensor_scalar(
                out=h_fm[:, kc, pos:pos + 128],
                in0=psT[:, kc * 128:(kc + 1) * 128],
                scalar1=gamf[:, kc:kc + 1], scalar2=betf[:, kc:kc + 1],
                op0=ALU.mult, op1=ALU.add),
               reads=["psT", "gamf", "betf"],
               writes=[f"h{kc}_{q}"] + (hal if kc == 0 else []))


    uB = [u_t, u_t2]
    xldB = [xld, xld2]
    otmB = [otm, otm2]
    ofmB = [ofm, ofm2]

    for i in range(NTC + 6):
        pump_dma(5, i)
        jx = i - 2
        if 0 <= jx < NTC:
            xal = (["kvr"] if jx == 0 else []) + (["otm2"] if jx == 1 else [])
            Yd(lambda e, jx=jx: e.dma_start(out=xldB[jx % 2][:, :],
                                            in_=x_d[jx * 128:(jx + 1) * 128, :]),
               reads=[], writes=[f"xld{jx % 2}"] + xal)
        if i < NTC:
            b = i // 4
            for h in range(H):
                col = head_col(h)
                base = (h % 2) * 64
                T_(lambda t, i=i, h=h, col=col, base=base, b=b: t.matmul(
                    psN[:, col:col + 65],
                    q_fm[base:base + 64, h // 2, i * 128:(i + 1) * 128],
                    kv_aug[base:base + 64, b * 1040 + h * 65:b * 1040 + (h + 1) * 65],
                    start=True, stop=True),
                   reads=[f"qf{h // 2}_{i // 8}", "kva_lo", "kva_hi"],
                   writes=[f"psN{h // 7}"])
        j = i - 2
        if 0 <= j < NTC:
            for kc in range(8):
                T_(lambda t, j=j, kc=kc: t.transpose(
                    psT[:, kc * 128:(kc + 1) * 128],
                    otmB[j % 2][:, kc * 128:(kc + 1) * 128], ident[:, :]),
                   reads=[f"otm{j % 2}", "ident"], writes=["psT"])
            eal = ["ofm2"] if j == 1 else []
            V_(lambda v, j=j: v.tensor_copy(out=ofmB[j % 2][:, :], in_=psT[:, :]),
               reads=["psT"], writes=[f"ofm{j % 2}"] + eal)
        if i < NTC:
            for jj, nh in [(0, 7), (1, 7), (2, 2)]:
                V_(lambda v, jj=jj, nh=nh: v.tensor_copy(
                    out=den[:, 7 * jj:7 * jj + nh],
                    in_=psN[:, jj * 512:jj * 512 + nh * 65].rearrange(
                        "p (h e) -> p h e", h=nh)[:, :, 64:65].squeeze(2)),
                   reads=[f"psN{jj}"], writes=["den"])
            V_(lambda v: v.tensor_scalar_max(out=den[:, :], in0=den[:, :],
                                             scalar1=1e-6),
               reads=["den"], writes=["den"])
            V_(lambda v: v.reciprocal(out=rr[:, :], in_=den[:, :]),
               reads=["den"], writes=["rr"])
            for jj, nh in [(0, 7), (1, 7), (2, 2)]:
                V_(lambda v, i=i, jj=jj, nh=nh: v.tensor_tensor(
                    out=otmB[i % 2][:, jj * 448:jj * 448 + nh * 64].rearrange(
                        "p (h e) -> p h e", h=nh),
                    in0=psN[:, jj * 512:jj * 512 + nh * 65].rearrange(
                        "p (h e) -> p h e", h=nh)[:, :, 0:64],
                    in1=rr[:, 7 * jj:7 * jj + nh].unsqueeze(2).broadcast_to(
                        [128, nh, 64]),
                    op=ALU.mult),
                   reads=[f"psN{jj}", "rr"], writes=[f"otm{i % 2}"])
        k = i - 3
        if 0 <= k < NTC:
            ps, pks = (psA, ["psA0", "psA1"]) if k % 2 == 0 else (psB, ["psB0", "psB1"])
            for half in range(2):
                sl = slice(half * 512, (half + 1) * 512)
                for kc in range(8):
                    T_(lambda t, k=k, kc=kc, sl=sl, ps=ps: t.matmul(
                        ps[:, sl], ofmB[k % 2][:, kc * 128:(kc + 1) * 128],
                        wob[:, kc, sl], start=(kc == 0), stop=False),
                       reads=[f"ofm{k % 2}", f"wo{kc}"], writes=[pks[half]],
                       cont=(kc != 0))
                T_(lambda t, sl=sl, ps=ps: t.matmul(
                    ps[:, sl], ones[0:1, 0:128], b2rows[0:1, 0, sl],
                    start=False, stop=True),
                   reads=["ones", "bohi"], writes=[pks[half]], cont=True)
            p = k % 2
            B0 = 32 + 16 * p
            V_(lambda v, k=k, ps=ps, p=p: v.tensor_tensor(
                out=uB[p][:, :], in0=ps[:, :], in1=xldB[p][:, :], op=ALU.add),
               reads=pks + [f"xld{p}"],
               writes=[f"u{p}"] + (["kva_lo", "kva_hi", "t0"]
                                    if k == 0 else []))
            V_(lambda v, p=p, B0=B0: v.reduce_sum(out=stat[:, B0:B0 + 1],
                                                  in_=uB[p][:, :], axis=AXX),
               reads=[f"u{p}"], writes=[f"sa{p}0"])
            S_(lambda sc, p=p, B0=B0: sc.activation(
                out=wstC[:, :], in_=uB[p][:, :], func=AF.Square,
                accum_out=stat[:, B0 + 2:B0 + 3]),
               reads=[f"u{p}"], writes=["wstC", f"sa{p}2"] +
                                       (["xfm6"] if k == 0 else []))
        m = i - 4
        if 0 <= m < NTC:
            q = m % 2
            B0 = 32 + 16 * q
            q2 = (m + 1) % 2
            B2 = 32 + 16 * q2
            V_(lambda v, q=q, B0=B0: v.tensor_scalar(
                out=x1t[:, :], in0=uB[q][:, :],
                scalar1=stat[:, B0 + 4:B0 + 5],
                scalar2=stat[:, B0 + 12:B0 + 13],
                op0=ALU.subtract, op1=ALU.mult),
               reads=[f"u{q}", f"sa{q}m", f"sa{q}r"], writes=["x1t"])
            S_(lambda sc, B2=B2: sc.activation(out=wstC[:, :], in_=x1t[:, :],
                                               func=AF.Identity,
                                               accum_out=stat[:, B2 + 1:B2 + 2]),
               reads=["x1t"], writes=["wstC", f"sa{q2}1"])
            S_(lambda sc, B2=B2: sc.activation(out=wstC[:, :], in_=x1t[:, :],
                                               func=AF.Square,
                                               accum_out=stat[:, B2 + 3:B2 + 4]),
               reads=["x1t"], writes=["wstC", f"sa{q2}3"])
            Pd(lambda e, m=m: e.dma_start(out=x1_d[m * 128:(m + 1) * 128, :],
                                          in_=x1t[:, :]),
               reads=["x1t"], writes=[f"x1d{m}"])
        kc2 = i - 3
        if 0 <= kc2 <= NTC:
            pc = kc2 % 2
            Bc = 32 + 16 * pc
            racc = ([f"sa{pc}0", f"sa{pc}1"] if kc2 < NTC else [f"sa{pc}1"])
            S_(lambda sc, Bc=Bc: sc.activation(
                out=stat[:, Bc + 4:Bc + 6], in_=stat[:, Bc:Bc + 2],
                func=AF.Identity, scale=1.0 / C),
               reads=racc, writes=[f"sa{pc}m"])
            S_(lambda sc, Bc=Bc: sc.activation(
                out=stat[:, Bc + 6:Bc + 8], in_=stat[:, Bc + 4:Bc + 6],
                func=AF.Square),
               reads=[f"sa{pc}m"], writes=[f"sa{pc}q"])
            V_(lambda v, Bc=Bc: v.scalar_tensor_tensor(
                out=stat[:, Bc + 8:Bc + 10], in0=stat[:, Bc + 2:Bc + 4],
                scalar=1.0 / C, in1=stat[:, Bc + 6:Bc + 8],
                op0=ALU.mult, op1=ALU.subtract),
               reads=[f"sa{pc}q", f"sa{pc}2", f"sa{pc}3"], writes=[f"sa{pc}n"])
            S_(lambda sc, Bc=Bc: sc.activation(
                out=stat[:, Bc + 10:Bc + 12], in_=stat[:, Bc + 8:Bc + 10],
                func=AF.Sqrt, bias=epsc[:, 0:1]),
               reads=[f"sa{pc}n", "epsc"], writes=[f"sa{pc}s"])
            V_(lambda v, Bc=Bc: v.reciprocal(
                out=stat[:, Bc + 12:Bc + 14], in_=stat[:, Bc + 10:Bc + 12]),
               reads=[f"sa{pc}s"], writes=[f"sa{pc}r"])
            mp = kc2 - 1
            if mp >= 0:
                V_(lambda v, Bc=Bc, mp=mp: v.tensor_copy(
                    out=stat[:, 2 * mp:2 * mp + 1], in_=stat[:, Bc + 5:Bc + 6]),
                   reads=[f"sa{pc}m"], writes=[f"mr_{mp}"])
                V_(lambda v, Bc=Bc, mp=mp: v.tensor_copy(
                    out=stat[:, 2 * mp + 1:2 * mp + 2],
                    in_=stat[:, Bc + 13:Bc + 14]),
                   reads=[f"sa{pc}r"], writes=[f"mr_{mp}"])
        pump_comp(5)
        pump_dma(3, i)
        if i == 18:
            load_tc(0)
            load_tc(1)
        if i == 19:
            while dq_state["nc"] < len(dq_state["jobs"]):
                pump_dma(99)
                pump_comp(99)
            emit_h_tc(0)
            load_tc(2)
        if i == 20:
            emit_h_tc(1)
            load_tc(3)
        if i == 21:
            emit_h_tc(2)
            emit_h_tc(3)
    while dq_state["nc"] < len(dq_state["jobs"]):
        pump_dma(2)
        pump_comp(2)

    def emit_fc2_tail(s8):
        tcs = [s8 * 2, s8 * 2 + 1]
        for i, tc in enumerate(tcs):
            sl5 = tc % 5
            q = (s8 % 2) * 2 + i
            for half in range(2):
                nk = f"psN{half}"
                sl = slice(half * 512, (half + 1) * 512)
                for fc in range(32):
                    T_(lambda t, fc=fc, sl=sl, q=q: t.matmul(
                        psN[:, sl], y_fm[:, fc, q * 128:(q + 1) * 128],
                        w2b[:, fc, sl], start=(fc == 0), stop=False),
                       reads=[f"yf{fc}", f"w2b{fc}"], writes=[nk],
                       cont=(fc != 0))
                T_(lambda t, sl=sl: t.matmul(
                    psN[:, sl], ones[0:1, 0:128], b2rows[0:1, 2, sl],
                    start=False, stop=True),
                   reads=["ones", "b2hi"], writes=[nk], cont=True)
            V_(lambda v, sl5=sl5: v.scalar_tensor_tensor(
                out=x1c[:, sl5, 0:512], in0=psN[:, 0:512], scalar=0.25,
                in1=x1c[:, sl5, 0:512], op0=ALU.mult, op1=ALU.add),
               reads=["psN0", f"x1c{sl5}"], writes=[f"x1c{sl5}"])
            V_(lambda v, sl5=sl5: v.scalar_tensor_tensor(
                out=x1c[:, sl5, 512:1024], in0=psN[:, 512:1024], scalar=0.25,
                in1=x1c[:, sl5, 512:1024], op0=ALU.mult, op1=ALU.add),
               reads=["psN1", f"x1c{sl5}"], writes=[f"x1c{sl5}"])
            V_(lambda v, i=i, sl5=sl5: v.reduce_sum(
                out=vb[:, i:i + 1], in_=x1c[:, sl5, :], axis=AXX),
               reads=[f"x1c{sl5}"], writes=[f"vb{i}"])
            S_(lambda sc, i=i, sl5=sl5: sc.activation(
                out=wstD[:, :], in_=x1c[:, sl5, :], func=AF.Square,
                accum_out=vb[:, 8 + i:9 + i]),
               reads=[f"x1c{sl5}"], writes=["wstD", f"vbs{i}"])
        for i in range(2):
            V_(lambda v, i=i: v.tensor_scalar_mul(
                out=vb[:, i:i + 1], in0=vb[:, i:i + 1], scalar1=1.0 / C),
               reads=[f"vb{i}"], writes=[f"vb{i}"])
            V_(lambda v, i=i: v.tensor_scalar_mul(
                out=vb[:, 8 + i:9 + i], in0=vb[:, 8 + i:9 + i], scalar1=1.0 / C),
               reads=[f"vbs{i}"], writes=[f"vbs{i}"])
            V_(lambda v, i=i: v.tensor_tensor(out=ss[:, 14:15],
                                              in0=vb[:, i:i + 1],
                                              in1=vb[:, i:i + 1], op=ALU.mult),
               reads=[f"vb{i}"], writes=["ss14"])
            V_(lambda v, i=i: v.tensor_tensor(out=vb[:, 8 + i:9 + i],
                                              in0=vb[:, 8 + i:9 + i],
                                              in1=ss[:, 14:15], op=ALU.subtract),
               reads=[f"vbs{i}", "ss14"], writes=[f"vbs{i}"])
        S_(lambda sc: sc.activation(out=vb[:, 12:14], in_=vb[:, 8:10],
                                    func=AF.Sqrt, bias=epsc[:, 1:2]),
           reads=["vbs0", "vbs1", "epsc"], writes=["sd4"])
        V_(lambda v: v.reciprocal(out=vb[:, 12:14], in_=vb[:, 12:14]),
           reads=["sd4"], writes=["sd4"])
        for i, tc in enumerate(tcs):
            sl5 = tc % 5
            V_(lambda v, i=i, sl5=sl5: v.tensor_scalar(
                out=x1c[:, sl5, :], in0=x1c[:, sl5, :],
                scalar1=vb[:, i:i + 1], scalar2=vb[:, 12 + i:13 + i],
                op0=ALU.subtract, op1=ALU.mult),
               reads=[f"x1c{sl5}", f"vb{i}", "sd4"], writes=[f"x1c{sl5}"])
            Yd(lambda e, tc=tc, sl5=sl5: e.dma_start(
                out=y_d[tc * 128:(tc + 1) * 128, :], in_=x1c[:, sl5, :]),
               reads=[f"x1c{sl5}"], writes=[f"yo{tc}"])

    for s4 in range(4):
        for nch in range(32):
            qsel = nch % 4
            pst = psA if qsel < 2 else psB
            bank = pst[:, (qsel % 2) * 512:(qsel % 2 + 1) * 512]
            bkey = ["psA0", "psA1", "psB0", "psB1"][qsel]
            nxt = ([["psA0", "psA1", "psB0", "psB1"][(qsel + 1) % 4]]
                   if nch % 2 == 0 else [])
            for kc in range(8):
                T_(lambda t, bank=bank, nch=nch, kc=kc: t.matmul(
                    bank, w1b[:, kc, nch * 128:(nch + 1) * 128],
                    h_fm[:, kc, :], start=(kc == 0), stop=(kc == 7)),
                   reads=([f"w1b{kc}"] + [f"h{kc}_{qq}" for qq in range(4)]
                          + (nxt if kc == 0 else [])),
                   writes=[bkey], cont=(kc != 0))
            S_(lambda sc, bank=bank, nch=nch: sc.activation(
                out=y_fm[:, nch, :], in_=bank, func=AF.Silu,
                bias=b1f[:, nch:nch + 1]),
               reads=[bkey, "b1f"],
               writes=[f"yf{nch}"] + (Y0AL if (s4 == 0 and nch == 0) else []))
        load_tc(4 * s4 + 4)
        emit_h_tc(4 * s4 + 4)
        emit_fc2_tail(2 * s4)
        load_tc(4 * s4 + 5)
        emit_h_tc(4 * s4 + 5)
        emit_fc2_tail(2 * s4 + 1)
        load_tc(4 * s4 + 6)
        emit_h_tc(4 * s4 + 6)
        load_tc(4 * s4 + 7)
        emit_h_tc(4 * s4 + 7)

    r.add("Y", lambda e: e.nop(), reads=[f"yo{tc}" for tc in range(NTC)],
          writes=[])

    ops = r.resolve()
    streams = {"T": [], "V": [], "S": [], "P": [], "Y": []}
    for op in ops:
        streams[op.eng].append(op)

    measured = {}

    def run_stream(eng_obj, lst):
        hw = {}
        for op in lst:
            for sem_key, val in op.waits:
                if hw.get(sem_key, 0) >= val:
                    continue
                eng_obj.wait_ge(sems[sem_key], val)
                hw[sem_key] = val
            if op.issue_wait is not None:
                skey, val = op.issue_wait
                if hw.get(skey, 0) < val:
                    eng_obj.wait_ge(sems[skey], val)
                    hw[skey] = val
            n0 = nc.n_instructions()
            ins = op.fn(eng_obj)
            if op.queue:
                measured[op.idx] = nc.n_instructions() - n0
            if op.inc:
                ins.then_inc(sems[op.qsem if op.queue else op.eng],
                             16 if op.queue else 1)

    with nc.Block() as blk:
        @blk.tensor
        def _(e):
            run_stream(e, streams["T"])

        @blk.vector
        def _(e):
            run_stream(e, streams["V"])

        @blk.scalar
        def _(e):
            run_stream(e, streams["S"])

        @blk.gpsimd
        def _(e):
            run_stream(e, streams["P"])

        @blk.sync
        def _(e):
            run_stream(e, streams["Y"])

    if dma_counts is None:
        return measured
    return nc


_NC_CACHE = [None]


def kernel(**inputs):
    inp = {k: np.asarray(v) for k, v in inputs.items()}
    x = inp["L0_attn_x"].astype(np.float32, copy=False)

    wkT = np.ascontiguousarray(inp["L0_attn_Wk"].T.astype(np.float32, copy=False))
    wvT = np.ascontiguousarray(inp["L0_attn_Wv"].T.astype(np.float32, copy=False))
    wqT = np.ascontiguousarray(inp["L0_attn_Wq"].T.astype(np.float32, copy=False))
    woT = np.ascontiguousarray(inp["L0_attn_Wo"].T.astype(np.float32, copy=False))
    brows = np.ascontiguousarray(np.stack([
        inp["L0_attn_bk"], inp["L0_attn_bv"], inp["L0_attn_bq"],
        inp["L0_attn_bo"], inp["L0_ffn_fc2_bias"]]).astype(np.float32))
    gamf = np.ascontiguousarray(
        inp["L0_ffn_ln_gamma"].astype(np.float32).reshape(8, 128).T)
    betf = np.ascontiguousarray(
        inp["L0_ffn_ln_beta"].astype(np.float32).reshape(8, 128).T)
    b1f = np.ascontiguousarray(
        inp["L0_ffn_fc1_bias"].astype(np.float32).reshape(32, 128).T)
    bqf = np.ascontiguousarray(
        inp["L0_attn_bq"].astype(np.float32).reshape(8, 128).T)
    q1 = np.ascontiguousarray(
        inp["L0_ffn_fc1_q"].astype(np.uint8, copy=False)
        .reshape(8, 128, 8, 512).transpose(0, 2, 1, 3).reshape(64 * 128, 512))
    s1 = np.ascontiguousarray(inp["L0_ffn_fc1_scales"].astype(np.float32, copy=False))
    q2 = np.ascontiguousarray(
        inp["L0_ffn_fc2_q"].astype(np.uint8, copy=False)
        .reshape(32, 128, 2, 512).transpose(0, 2, 1, 3).reshape(64 * 128, 512))
    s2 = np.ascontiguousarray(inp["L0_ffn_fc2_scales"].astype(np.float32, copy=False))

    if _NC_CACHE[0] is None:
        _NC_CACHE[0] = build_nc()
    nc = _NC_CACHE[0]

    in_maps = []
    for c in range(N_CORES):
        xc = np.ascontiguousarray(x[:, c * 512:(c + 1) * 512, :].reshape(M, C))
        in_maps.append(dict(
            x=xc, xT=np.ascontiguousarray(xc.T),
            wkT=wkT, wvT=wvT, wqT=wqT, woT=woT, brows=brows,
            gamf=gamf, betf=betf, b1f=b1f, bqf=bqf,
            q1=q1, s1=s1, q2=q2, s2=s2))

    res = run_bass_kernel_spmd(nc, in_maps, core_ids=list(range(N_CORES)))
    out = np.empty((B, T, C), np.float32)
    for c in range(N_CORES):
        out[:, c * 512:(c + 1) * 512, :] = res.results[c]["y"].reshape(B, 512, C)
    return out


# version 28
# speedup vs baseline: 1.5127x; 1.0074x over previous
import numpy as np
import concourse.bass as bass
import concourse.mybir as mybir
from concourse.bass_utils import run_bass_kernel_spmd

F32 = mybir.dt.float32
BF16 = mybir.dt.bfloat16
I32 = mybir.dt.int32
U8 = mybir.dt.uint8
AF = mybir.ActivationFunctionType
ALU = mybir.AluOpType
AXX = mybir.AxisListType.X

KB = 1024
N_CORES = 8
B, T, C = 4, 4096, 1024
H = 16
M = 2048
NTC = 16
F = 4096
EPS = 1e-5


def head_col(h):
    return (h // 7) * 512 + (h % 7) * 65


QROT = 8
SAFE_EDGES = True


class Op:
    __slots__ = ("eng", "fn", "reads", "writes", "queue", "idx", "inc",
                 "inc_val", "waits", "deps", "qsem", "issue_wait", "cont",
                 "guar")

    def __init__(self, eng, fn, reads, writes, queue, cont=False):
        self.eng = eng
        self.fn = fn
        self.reads = tuple(reads)
        self.writes = tuple(writes)
        self.queue = queue
        self.inc = False
        self.inc_val = None
        self.waits = []
        self.deps = []
        self.qsem = None
        self.issue_wait = None
        self.cont = cont
        self.guar = {}


class Rec:
    def __init__(self, dma_counts=None):
        self.ops = []
        self.dma_counts = dma_counts

    def add(self, eng, fn, reads=(), writes=(), dma=False, cont=False):
        self.ops.append(Op(eng, fn, reads, writes,
                           eng + "q" if dma else None, cont))

    def resolve(self):
        ops = self.ops
        for i, op in enumerate(ops):
            op.idx = i
        quse = {}
        scount = {}
        for op in ops:
            if op.queue is not None:
                op.inc = True
                i = quse.get(op.queue, 0)
                quse[op.queue] = i + 1
                skey = f"{op.queue}{i % QROT}"
                op.qsem = skey
                prev = scount.get(skey, 0)
                op.issue_wait = (skey, prev) if prev else None
                k = self.dma_counts.get(op.idx, 1) if self.dma_counts else 1
                scount[skey] = prev + 16 * k
                op.inc_val = scount[skey]
        writer = {}
        readers = {}
        for op in ops:
            deps = {}

            def dep_on(d, op=op, deps=deps):
                if d is None or d is op:
                    return
                src = d.qsem if d.queue else d.eng
                if d.queue is None and op.queue is not None and d.eng == op.eng:
                    return
                if (d.queue is None and op.queue is None
                        and d.eng == "T" and op.eng == "T" and op.cont):
                    return
                if (not SAFE_EDGES and d.queue is None and op.queue is None
                        and d.eng == op.eng and d.eng in ("V", "S")):
                    return
                cur = deps.get(src)
                if cur is None or d.idx > cur.idx:
                    deps[src] = d

            for b_ in op.reads:
                dep_on(writer.get(b_))
            for b_ in op.writes:
                dep_on(writer.get(b_))
                for rd in readers.get(b_, {}).values():
                    dep_on(rd)
            op.deps = list(deps.values())
            for d in op.deps:
                d.inc = True
            for b_ in op.reads:
                readers.setdefault(b_, {})[op.queue or op.eng] = op
            for b_ in op.writes:
                writer[b_] = op
                readers[b_] = {}
        count = {}
        for op in ops:
            if op.queue is None and op.inc:
                key = op.eng
                count[key] = count.get(key, 0) + 1
                op.inc_val = count[key]
        stream_cur = {}
        for op in ops:
            me = op.eng
            cur = stream_cur.setdefault(me, {})
            for d in sorted(op.deps, key=lambda d: -d.idx):
                key = d.qsem if d.queue else d.eng
                if cur.get(key, 0) < d.inc_val:
                    op.waits.append((key, d.inc_val))
                    cur[key] = d.inc_val
                for k2, v2 in d.guar.items():
                    if cur.get(k2, 0) < v2:
                        cur[k2] = v2
            g = dict(cur)
            if op.inc:
                k = op.qsem if op.queue else op.eng
                g[k] = max(g.get(k, 0), op.inc_val)
            op.guar = g
        return ops


def build_nc():
    counts = _build_inner(None)
    return _build_inner(counts)


def _build_inner(dma_counts):
    nc = bass.Bass("TRN2", target_bir_lowering=False, debug=False)

    x_d = nc.dram_tensor("x", [M, C], F32, kind="ExternalInput")
    xT_d = nc.dram_tensor("xT", [C, M], F32, kind="ExternalInput")
    wkT_d = nc.dram_tensor("wkT", [C, C], F32, kind="ExternalInput")
    wvT_d = nc.dram_tensor("wvT", [C, C], F32, kind="ExternalInput")
    wqT_d = nc.dram_tensor("wqT", [C, C], F32, kind="ExternalInput")
    woT_d = nc.dram_tensor("woT", [C, C], F32, kind="ExternalInput")
    brows_d = nc.dram_tensor("brows", [5, C], F32, kind="ExternalInput")
    gamf_d = nc.dram_tensor("gamf", [128, 8], F32, kind="ExternalInput")
    betf_d = nc.dram_tensor("betf", [128, 8], F32, kind="ExternalInput")
    b1f_d = nc.dram_tensor("b1f", [128, 32], F32, kind="ExternalInput")
    bqf_d = nc.dram_tensor("bqf", [128, 8], F32, kind="ExternalInput")
    q1_d = nc.dram_tensor("q1", [64 * 128, 512], U8, kind="ExternalInput")
    s1_d = nc.dram_tensor("s1", [8, F], F32, kind="ExternalInput")
    q2_d = nc.dram_tensor("q2", [64 * 128, 512], U8, kind="ExternalInput")
    s2_d = nc.dram_tensor("s2", [32, C], F32, kind="ExternalInput")
    y_d = nc.dram_tensor("y", [M, C], F32, kind="ExternalOutput")

    x1_d = nc.dram_tensor("x1spill", [M, C], F32)
    cc_in = nc.dram_tensor("cc_in", [64, 4160], F32)
    cc_out = nc.dram_tensor("cc_out", [64, 4160], F32)

    A = nc.alloc_sbuf_tensor_at
    O = 16 * KB
    ident = A("ident", [128, 128], BF16, offset=O + 1024)
    ones = A("ones", [1, 512], BF16, offset=O + 1280)
    gamf = A("gamf_s", [128, 8], F32, offset=O + 2304)
    betf = A("betf_s", [128, 8], F32, offset=O + 2336)
    b1f = A("b1f_s", [128, 32], F32, offset=O + 2368)
    stat = A("stat", [128, 64], F32, offset=O + 2496)
    bqf = A("bqf_s", [128, 8], F32, offset=O + 3072)
    ss = A("ss", [128, 16], F32, offset=O + 2752)
    vb = A("vb", [128, 16], F32, offset=O + 2816)
    epsc = A("epsc", [128, 4], F32, offset=O + 2880)
    den = A("den", [128, 16], F32, offset=O + 2944)
    rr = A("rr", [128, 16], F32, offset=O + 3008)
    b2rows = A("b2rows", [1, 4, 1024], BF16, offset=20 * KB)
    b1rows = A("b1rows", [1, 6, 1024], BF16, offset=28 * KB)
    qst = A("qst", [128, 4, 512], U8, offset=28 * KB)
    sbc = A("sbc", [128, 4, 512], F32, offset=30 * KB)
    u_t2 = A("u_t2", [128, 1024], F32, offset=152 * KB)
    xnt = A("xnt", [128, 1024], BF16, offset=38 * KB)
    wstC = A("wstC", [128, 1024], BF16, offset=129 * KB)
    wstD = A("wstD", [128, 1024], BF16, offset=156 * KB)
    xld2 = A("xld2", [128, 1024], F32, offset=131 * KB)
    ofm2 = A("ofm2", [128, 1024], BF16, offset=22 * KB)
    otm2 = A("otm2", [128, 1024], BF16, offset=26 * KB)
    t1q = A("t1q", [128, 1024], F32, offset=72 * KB)
    t2q = A("t2q", [128, 1024], F32, offset=76 * KB)
    stgB = A("stgB", [128, 2048], F32, offset=48 * KB)
    x1c = A("x1c", [128, 5, 1024], F32, offset=136 * KB)
    stgA = A("stgA", [128, 2048], F32, offset=40 * KB)
    stgC = A("stgC", [128, 2048], F32, offset=72 * KB)
    stgD = A("stgD", [128, 2048], F32, offset=80 * KB)
    wsg = A("wsg", [128, 4, 1024], F32, offset=159 * KB)
    brf = A("brf", [1, 1024], F32, offset=100 * KB)
    brf2 = A("brf2", [1, 1024], F32, offset=199 * KB)
    t1 = A("t1", [128, 1024], F32, offset=48 * KB)
    t2 = A("t2", [128, 1024], F32, offset=52 * KB)
    wkb = A("wkb", [128, 8, 1024], BF16, offset=56 * KB)
    wqb = A("wqb", [128, 8, 1024], BF16, offset=56 * KB)
    ktiles = A("ktiles", [128, 4, 1024], BF16, offset=72 * KB)
    v_aug = A("v_aug", [128, 4, 1040], BF16, offset=80 * KB)
    kv_stage = A("kv_stage", [64, 2, 1040], F32, offset=88 * KB + 256)
    w1b = A("w1b", [128, 8, 4096], BF16, offset=40 * KB)
    x_fm = A("x_fm", [128, 8, 2048], BF16, offset=104 * KB)
    kvr = A("kvr", [64, 4160], F32, offset=104 * KB)
    xld = A("xld", [128, 1024], F32, offset=104 * KB)
    ofm = A("ofm", [128, 1024], BF16, offset=108 * KB)
    otm = A("otm", [128, 1024], BF16, offset=110 * KB)
    u_t = A("u_t", [128, 1024], F32, offset=112 * KB)
    x1t = A("x1t", [128, 1024], F32, offset=116 * KB)
    kv_aug = A("kv_aug", [128, 4160], BF16, offset=120 * KB + 512)
    y_fm = A("y_fm", [128, 32, 512], BF16, offset=104 * KB)
    h_fm = A("h_fm", [128, 8, 512], BF16, offset=28 * KB)
    wvb = A("wvb", [128, 8, 1024], BF16, offset=136 * KB)
    wob = A("wob", [128, 8, 1024], BF16, offset=136 * KB)
    w2b = A("w2b", [128, 32, 1024], BF16, offset=159 * KB)
    q_fm = A("q_fm", [128, 8, 2048], BF16, offset=191 * KB)
    bk_bc = A("bk_bc", [128, 1024], F32, offset=191 * KB)
    bv_bc = A("bv_bc", [128, 1024], F32, offset=195 * KB)
    t0p = A("t0p", [128, 1024], F32, offset=199 * KB)
    t0q = A("t0q", [128, 1024], F32, offset=48 * KB)

    psA = nc.alloc_psum_tensor("psA", [128, 1024], F32)
    psB = nc.alloc_psum_tensor("psB", [128, 1024], F32)
    psN = nc.alloc_psum_tensor("psN", [128, 1536], F32)
    psT = nc.alloc_psum_tensor("psT", [128, 1024], BF16)

    sem_keys = ["T", "V", "S", "P", "Y"]
    for q in ["Yq", "Sq", "Pq"]:
        sem_keys += [f"{q}{i}" for i in range(QROT)]
    sems = {k: nc.alloc_semaphore("sem_" + k) for k in sem_keys}

    r = Rec(dma_counts)
    T_ = lambda fn, reads=(), writes=(), cont=False: r.add("T", fn, reads,
                                                           writes, cont=cont)
    V_ = lambda fn, reads=(), writes=(): r.add("V", fn, reads, writes)
    S_ = lambda fn, reads=(), writes=(): r.add("S", fn, reads, writes)
    P_ = lambda fn, reads=(), writes=(): r.add("P", fn, reads, writes)
    Yd = lambda fn, reads=(), writes=(): r.add("Y", fn, reads, writes, dma=True)
    Sd = lambda fn, reads=(), writes=(): r.add("S", fn, reads, writes, dma=True)
    Pd = lambda fn, reads=(), writes=(): r.add("P", fn, reads, writes, dma=True)

    XFM = [f"xfm{c}" for c in range(8)]
    VAK = [f"va{i}" for i in range(4)]
    KVS = ["kvs0", "kvs1"]
    WOK = [f"wo{c}" for c in range(8)]

    P_(lambda g: g.memset(ident[:, :], 0.0), writes=["ident", "ptok"])
    P_(lambda g: g.affine_select(out=ident[:, :], in_=ident[:, :],
                                 compare_op=ALU.not_equal, fill=1.0, base=0,
                                 pattern=[[-1, 128]], channel_multiplier=1),
       reads=["ident", "ptok"], writes=["ident", "ptok"])
    P_(lambda g: g.memset(ones[:, :], 1.0), writes=["ones", "ptok"])
    P_(lambda g: g.memset(epsc[:, 0:1], EPS), writes=["epsc", "ptok"])
    P_(lambda g: g.memset(epsc[:, 1:2], EPS / 4.0), writes=["epsc", "ptok"])
    P_(lambda g: g.memset(epsc[:, 2:3], -128.0), writes=["epsc", "ptok"])
    P_(lambda g: g.memset(stat[:, 32:64], 1024.0),
       writes=["ptok"] + [f"sa{p}{c}" for p in range(2) for c in "0123"])

    Yd(lambda e: e.dma_start(out=gamf[:, :], in_=gamf_d[:, :]), writes=["gamf"])
    Yd(lambda e: e.dma_start(out=betf[:, :], in_=betf_d[:, :]), writes=["betf"])
    Yd(lambda e: e.dma_start(out=b1f[:, :], in_=b1f_d[:, :]), writes=["b1f"])
    Yd(lambda e: e.dma_start(out=bqf[:, :], in_=bqf_d[:, :]), writes=["bqf"])
    BRF = [brf, brf2]
    for j, (row, tgt, slot, nm) in enumerate(
            [(0, b1rows, 0, "bk"), (1, b1rows, 2, "bv"),
             (3, b2rows, 0, "bo"), (4, b2rows, 2, "b2")]):
        hi = nm + "hi"
        bslot = j % 2
        Yd(lambda e, row=row, bslot=bslot: e.dma_start(
            out=BRF[bslot][:, :], in_=brows_d[row:row + 1, :]),
           writes=[f"brf{bslot}"])
        S_(lambda sc, tgt=tgt, slot=slot, bslot=bslot: sc.copy(
            out=tgt[:, slot, :], in_=BRF[bslot][:, :]),
           reads=[f"brf{bslot}"], writes=[hi])

    XSTG = [stgA, stgB, stgC, stgD]
    for c in range(8):
        stg, skey = XSTG[c % 4], f"xsg{c % 4}"
        dq = Yd if c % 2 == 0 else Sd
        dq(lambda e, c=c, stg=stg: e.dma_start(
            out=stg[:, :], in_=xT_d[c * 128:(c + 1) * 128, :]),
           reads=[], writes=[skey])
        V_(lambda v, c=c, stg=stg: v.tensor_copy(out=x_fm[:, c, :],
                                                 in_=stg[:, :]),
           reads=[skey], writes=[f"xfm{c}"])
    def load_w(dram, dst, dstkey, alias=None):
        for c in range(8):
            slot = c % 4
            Pd(lambda e, c=c, slot=slot, dram=dram: e.dma_start(
                out=wsg[:, slot, :], in_=dram[c * 128:(c + 1) * 128, :]),
               writes=[f"wsg{slot}"])
            al = [f"{alias}{c}"] if alias else []
            S_(lambda sc, c=c, slot=slot, dst=dst: sc.copy(
                out=dst[:, c, :], in_=wsg[:, slot, :]),
               reads=[f"wsg{slot}"], writes=[f"{dstkey}{c}"] + al)

    load_w(wkT_d, wkb, "wk")
    load_w(wvT_d, wvb, "wv")
    P_(lambda g: g.memset(v_aug[:, :, :], 1.0),
       writes=VAK + ["ptok", "xsg3"])

    for half in range(2):
        sl = slice(half * 512, (half + 1) * 512)
        T_(lambda t, sl=sl: t.matmul(psA[:, sl], ones[0:1, 0:128],
                                     b1rows[0:1, 0, sl], start=True, stop=True),
           reads=["ones", "bkhi"], writes=["psA0" if half == 0 else "psA1"])
        T_(lambda t, sl=sl: t.matmul(psB[:, sl], ones[0:1, 0:128],
                                     b1rows[0:1, 2, sl], start=True, stop=True),
           reads=["ones", "bvhi"], writes=["psB0" if half == 0 else "psB1"])
    V_(lambda v: v.tensor_copy(out=bk_bc[:, :], in_=psA[:, :]),
       reads=["psA0", "psA1"], writes=["bkbc"])
    V_(lambda v: v.tensor_copy(out=bv_bc[:, :], in_=psB[:, :]),
       reads=["psB0", "psB1"], writes=["bvbc"])

    for b in range(B):
        for i in range(4):
            tc = 4 * b + i
            for kc in range(8):
                for half in range(2):
                    sl = slice(half * 512, (half + 1) * 512)
                    T_(lambda t, kc=kc, sl=sl, tc=tc: t.matmul(
                        psA[:, sl], x_fm[:, kc, tc * 128:(tc + 1) * 128],
                        wkb[:, kc, sl], start=(kc == 0), stop=(kc == 7)),
                       reads=[f"xfm{kc}", f"wk{kc}"], writes=[f"psA{half}"],
                       cont=not (kc == 0 and half == 0))
            kal = ["xsg1"] if tc == 0 else []
            V_(lambda v: v.tensor_tensor(out=t0p[:, :], in0=psA[:, :],
                                         in1=bk_bc[:, :], op=ALU.add),
               reads=["psA0", "psA1", "bkbc"],
               writes=["t0"] + (["brf1"] if tc == 0 else []))
            S_(lambda sc: sc.activation(out=t1[:, :], in_=t0p[:, :],
                                        func=AF.Relu, scale=-1.0),
               reads=["t0"], writes=["t1"] + kal)
            S_(lambda sc: sc.activation(out=t2[:, :], in_=t1[:, :],
                                        func=AF.Exp, scale=-1.0),
               reads=["t1"], writes=["t2"] + kal)
            V_(lambda v, i=i: v.scalar_tensor_tensor(
                out=ktiles[:, i, :], in0=t0p[:, :], scalar=0.0,
                in1=t2[:, :], op0=ALU.max, op1=ALU.add),
               reads=["t0", "t2"],
               writes=[f"kt{i}"] + (["xsg2"] if tc < 4 else []))
            for kc in range(8):
                for half in range(2):
                    sl = slice(half * 512, (half + 1) * 512)
                    T_(lambda t, kc=kc, sl=sl, tc=tc: t.matmul(
                        psB[:, sl], x_fm[:, kc, tc * 128:(tc + 1) * 128],
                        wvb[:, kc, sl], start=(kc == 0), stop=(kc == 7)),
                       reads=[f"xfm{kc}", f"wv{kc}"], writes=[f"psB{half}"],
                       cont=not (kc == 0 and half == 0))
            V_(lambda v: v.tensor_tensor(out=t0p[:, :], in0=psB[:, :],
                                         in1=bv_bc[:, :], op=ALU.add),
               reads=["psB0", "psB1", "bvbc"], writes=["t0"])
            S_(lambda sc, i=i: sc.activation(
                out=v_aug[:, i, :].rearrange("p (h e) -> p h e", h=16)[:, :, 0:64],
                in_=t0p[:, :].rearrange("p (h e) -> p h e", h=16),
                func=AF.Identity),
               reads=["t0"], writes=[f"va{i}"])
        for h in range(H):
            col = head_col(h)
            for i in range(4):
                T_(lambda t, h=h, i=i, col=col: t.matmul(
                    psN[0:64, col:col + 65],
                    ktiles[:, i, h * 64:(h + 1) * 64],
                    v_aug[:, i, h * 65:(h + 1) * 65],
                    start=(i == 0), stop=(i == 3)),
                   reads=[f"kt{i}", f"va{i}"], writes=[f"psN{h // 7}"],
                   cont=(i != 0))
        buf = b % 2
        for j, nh in [(0, 7), (1, 7), (2, 2)]:
            V_(lambda v, buf=buf, j=j, nh=nh: v.tensor_copy(
                out=kv_stage[:, buf, :].rearrange(
                    "p (h e) -> p h e", h=16)[:, 7 * j:7 * j + nh, :],
                in_=psN[0:64, j * 512:j * 512 + nh * 65].rearrange(
                    "p (h e) -> p h e", h=nh)),
               reads=[f"psN{j}"], writes=[f"kvs{buf}"])
        Yd(lambda e, b=b, buf=buf: e.dma_start(
            out=cc_in[:, b * 1040:(b + 1) * 1040],
            in_=kv_stage[:, buf, :]),
           reads=[f"kvs{buf}"], writes=[f"ccin{b}"])

    KTK = [f"kt{i}" for i in range(4)]
    W1AL = {0: ["xsg0"], 1: ["t1", "t2", "xsg1", "t0q"],
            2: ["wq0", "wq1", "wq2", "wq3"], 3: ["wq4", "wq5", "wq6", "wq7"],
            4: KTK + ["t1q", "t2q", "xsg2"], 5: VAK + ["xsg3"],
            6: VAK + KVS, 7: KVS}

    def dq_aps(job):
        kind, ch, s = job
        if kind == "w1":
            qsrc = q1_d[(ch * 8 + s) * 128:(ch * 8 + s + 1) * 128, :]
            ssrc = s1_d[ch:ch + 1, s * 512:(s + 1) * 512]
            dst = w1b[:, ch, s * 512:(s + 1) * 512]
            dkey = f"w1b{ch}"
            al = W1AL.get(ch, []) if s == 0 else []
        else:
            qsrc = q2_d[(ch * 2 + s) * 128:(ch * 2 + s + 1) * 128, :]
            ssrc = s2_d[ch:ch + 1, s * 512:(s + 1) * 512]
            dst = w2b[:, ch, s * 512:(s + 1) * 512]
            dkey = f"w2b{ch}"
            al = ([f"qf{(ch - 16) // 2}_{(ch - 16) % 2}"]
                  if (ch >= 16 and s == 0) else
                  [f"wsg{ch // 2}"] if (ch < 8 and s == 0) else [])
        return qsrc, ssrc, dst, dkey, al

    def emit_dq_dma(job, slot, first=False):
        qsrc, ssrc, dst, dkey, al = dq_aps(job)
        qk, sk = f"qst{slot}", f"sbc{slot}"
        qal = ["bkhi"] if first else []
        sal = ["bvhi"] if first else []
        Yd(lambda e, qsrc=qsrc, slot=slot: e.dma_start(
            out=qst[:, slot, :], in_=qsrc), writes=[qk] + qal)
        Yd(lambda e, ssrc=ssrc, slot=slot: e.dma_start(
            out=sbc[:, slot, :], in_=ssrc.partition_broadcast(128)),
           writes=[sk] + sal)

    def emit_dq_comp(job, slot):
        qsrc, ssrc, dst, dkey, al = dq_aps(job)
        qk, sk = f"qst{slot}", f"sbc{slot}"
        V_(lambda v, dst=dst, slot=slot: v.scalar_tensor_tensor(
            out=dst, in0=qst[:, slot, :], scalar=128.0,
            in1=sbc[:, slot, :], op0=ALU.subtract, op1=ALU.mult),
           reads=[qk, sk], writes=[dkey] + al)

    dq_p2 = ([("w1", kc, s) for kc in (5, 6, 7, 0) for s in range(8)] +
             [("w2", fc, s) for fc in range(8, 16) for s in range(2)])
    dq_c = ([("w1", kc, s) for kc in (1, 2, 3, 4) for s in range(8)] +
            [("w2", fc, s) for fc in range(8) for s in range(2)] +
            [("w2", fc, s) for fc in range(16, 32, 2) for s in range(2)] +
            [("w2", fc, s) for fc in range(17, 32, 2) for s in range(2)])
    dq_gates = ([0] * 48 + [9] * 16 + [16] * 16)
    dq_state = {"nd": 0, "nc": 0, "tot0": 0, "jobs": dq_p2,
                "gates": [0] * len(dq_p2)}

    def dq_switch(jobs, gates=None):
        st = dq_state
        assert st["nd"] == st["nc"] == len(st["jobs"])
        st["tot0"] += len(st["jobs"])
        st["jobs"] = jobs
        st["gates"] = gates if gates is not None else [0] * len(jobs)
        st["nd"] = st["nc"] = 0

    def pump_dma(k, it=10 ** 9):
        st = dq_state
        while (k > 0 and st["nd"] < len(st["jobs"])
               and st["nd"] < st["nc"] + 4
               and st["gates"][st["nd"]] <= it):
            j = st["nd"]
            emit_dq_dma(st["jobs"][j], (st["tot0"] + j) % 4,
                        first=(st["tot0"] + j == 0))
            st["nd"] += 1
            k -= 1

    def pump_comp(k):
        st = dq_state
        while k > 0 and st["nc"] < min(st["nd"], len(st["jobs"])):
            j = st["nc"]
            emit_dq_comp(st["jobs"][j], (st["tot0"] + j) % 4)
            st["nc"] += 1
            k -= 1

    def pump(k):
        pump_dma(k)
        pump_comp(k)

    load_w(wqT_d, wqb, "wq", alias="wk")
    load_w(woT_d, wob, "wo", alias="wv")

    P_(lambda g: g.collective_compute(
        "AllReduce", ALU.add, replica_groups=[list(range(N_CORES))],
        ins=[cc_in[:, :]], outs=[cc_out[:, :]]),
       reads=[f"ccin{b_}" for b_ in range(B)] + ["ptok"],
       writes=["ccout", "ptok"])

    for nch in range(8):
        for strip in range(2):
            ps, pks = (psA, ["psA0", "psA1"]) if (nch * 2 + strip) % 2 == 0 \
                else (psB, ["psB0", "psB1"])
            for kc in range(8):
                for half in range(2):
                    sl = slice(half * 512, (half + 1) * 512)
                    t0 = strip * 1024 + half * 512
                    T_(lambda t, ps=ps, nch=nch, kc=kc, sl=sl, t0=t0: t.matmul(
                        ps[:, sl], wqb[:, kc, nch * 128:(nch + 1) * 128],
                        x_fm[:, kc, t0:t0 + 512],
                        start=(kc == 0), stop=(kc == 7)),
                       reads=[f"wq{kc}", f"xfm{kc}"], writes=[pks[half]],
                       cont=not (kc == 0 and half == 0))
            qa1 = ["kt0", "kt1"] if (nch == 0 and strip == 0) else []
            qa2 = ["kt2", "kt3"] if (nch == 0 and strip == 0) else []
            qa0 = (["t1", "xsg1"] if (nch == 0 and strip == 0) else [])
            V_(lambda v, ps=ps, nch=nch: v.tensor_scalar_add(
                out=t0q[:, :], in0=ps[:, :], scalar1=bqf[:, nch:nch + 1]),
               reads=pks + ["bqf"], writes=["t0q"] + qa0)
            S_(lambda sc: sc.activation(out=t1q[:, :], in_=t0q[:, :],
                                        func=AF.Relu, scale=-1.0),
               reads=["t0q"], writes=["t1q"] + qa1)
            S_(lambda sc: sc.activation(out=t2q[:, :], in_=t1q[:, :],
                                        func=AF.Exp, scale=-1.0),
               reads=["t1q"], writes=["t2q"] + qa2)
            V_(lambda v, nch=nch, strip=strip: v.scalar_tensor_tensor(
                out=q_fm[:, nch, strip * 1024:(strip + 1) * 1024],
                in0=t0q[:, :], scalar=0.0, in1=t2q[:, :],
                op0=ALU.max, op1=ALU.add),
               reads=["t0q", "t2q"],
               writes=[f"qf{nch}_{strip}"] +
                      (["bkbc", "bvbc"] if (nch == 0 and strip == 0) else []) +
                      (["t0"] if (nch == 2 and strip == 0) else []))
            pump_dma(2)
            pump_comp(2)

    while dq_state["nc"] < len(dq_state["jobs"]):
        pump_dma(2)
        pump_comp(2)
    dq_switch(dq_c, dq_gates)

    Yd(lambda e: e.dma_start(out=kvr[:, :], in_=cc_out[:, :]),
       reads=["ccout"], writes=["kvr"] + XFM)
    V_(lambda v: v.tensor_copy(out=kv_aug[0:64, :], in_=kvr[:, :]),
       reads=["kvr"], writes=["kva_lo"])
    Pd(lambda e: e.dma_start(out=kv_aug[64:128, :], in_=kv_aug[0:64, :]),
       reads=["kva_lo"], writes=["kva_hi"])

    QSB = [f"qst{j}" for j in range(4)] + [f"sbc{j}" for j in range(3)]
    Y0AL = ["xld0", "ofm0", "otm0", "u0", "x1t", "kva_lo", "kva_hi",
            "wstC", "xld1"]
    X1AL = {0: ["wo0", "wo1"], 1: ["wo2", "wo3"], 2: ["wo4", "wo5"],
            3: ["wo6", "wo7"], 4: ["u1"]}

    def load_tc(tc):
        if not 0 <= tc < NTC:
            return
        sl5 = tc % 5
        x1al = X1AL.get(sl5, []) if tc < 5 else []
        Pd(lambda e, tc=tc, sl5=sl5: e.dma_start(
            out=x1c[:, sl5, :], in_=x1_d[tc * 128:(tc + 1) * 128, :]),
           reads=[f"x1d{tc}"], writes=[f"x1c{sl5}"] + x1al)

    def emit_h_tc(tc):
        if not 0 <= tc < NTC:
            return
        sl5 = tc % 5
        q = tc % 4
        pos = q * 128
        V_(lambda v, tc=tc, sl5=sl5: v.tensor_scalar(
            out=xnt[:, :], in0=x1c[:, sl5, :],
            scalar1=stat[:, 2 * tc:2 * tc + 1],
            scalar2=stat[:, 2 * tc + 1:2 * tc + 2],
            op0=ALU.subtract, op1=ALU.mult),
           reads=[f"x1c{sl5}", f"mr_{tc}"], writes=["xnt"])
        for kc in range(8):
            T_(lambda t, kc=kc: t.transpose(
                psT[:, kc * 128:(kc + 1) * 128],
                xnt[:, kc * 128:(kc + 1) * 128], ident[:, :]),
               reads=["xnt", "ident"], writes=["psT"])
        hal = QSB if tc == 0 else []
        for kc in range(8):
            V_(lambda v, kc=kc, pos=pos, q=q: v.tensor_scalar(
                out=h_fm[:, kc, pos:pos + 128],
                in0=psT[:, kc * 128:(kc + 1) * 128],
                scalar1=gamf[:, kc:kc + 1], scalar2=betf[:, kc:kc + 1],
                op0=ALU.mult, op1=ALU.add),
               reads=["psT", "gamf", "betf"],
               writes=[f"h{kc}_{q}"] + (hal if kc == 0 else []))


    uB = [u_t, u_t2]
    xldB = [xld, xld2]
    otmB = [otm, otm2]
    ofmB = [ofm, ofm2]

    for i in range(NTC + 6):
        pump_dma(5, i)
        jx = i - 2
        if 0 <= jx < NTC:
            xal = (["kvr"] if jx == 0 else []) + (["otm2"] if jx == 1 else [])
            Yd(lambda e, jx=jx: e.dma_start(out=xldB[jx % 2][:, :],
                                            in_=x_d[jx * 128:(jx + 1) * 128, :]),
               reads=[], writes=[f"xld{jx % 2}"] + xal)
        if i < NTC:
            b = i // 4
            for h in range(H):
                col = head_col(h)
                base = (h % 2) * 64
                T_(lambda t, i=i, h=h, col=col, base=base, b=b: t.matmul(
                    psN[:, col:col + 65],
                    q_fm[base:base + 64, h // 2, i * 128:(i + 1) * 128],
                    kv_aug[base:base + 64, b * 1040 + h * 65:b * 1040 + (h + 1) * 65],
                    start=True, stop=True),
                   reads=[f"qf{h // 2}_{i // 8}", "kva_lo", "kva_hi"],
                   writes=[f"psN{h // 7}"])
        j = i - 2
        if 0 <= j < NTC:
            for kc in range(8):
                T_(lambda t, j=j, kc=kc: t.transpose(
                    psT[:, kc * 128:(kc + 1) * 128],
                    otmB[j % 2][:, kc * 128:(kc + 1) * 128], ident[:, :]),
                   reads=[f"otm{j % 2}", "ident"], writes=["psT"])
            eal = ["ofm2"] if j == 1 else []
            V_(lambda v, j=j: v.tensor_copy(out=ofmB[j % 2][:, :], in_=psT[:, :]),
               reads=["psT"], writes=[f"ofm{j % 2}"] + eal)
        if i < NTC:
            for jj, nh in [(0, 7), (1, 7), (2, 2)]:
                V_(lambda v, jj=jj, nh=nh: v.tensor_copy(
                    out=den[:, 7 * jj:7 * jj + nh],
                    in_=psN[:, jj * 512:jj * 512 + nh * 65].rearrange(
                        "p (h e) -> p h e", h=nh)[:, :, 64:65].squeeze(2)),
                   reads=[f"psN{jj}"], writes=["den"])
            V_(lambda v: v.tensor_scalar_max(out=den[:, :], in0=den[:, :],
                                             scalar1=1e-6),
               reads=["den"], writes=["den"])
            V_(lambda v: v.reciprocal(out=rr[:, :], in_=den[:, :]),
               reads=["den"], writes=["rr"])
            for jj, nh in [(0, 7), (1, 7), (2, 2)]:
                V_(lambda v, i=i, jj=jj, nh=nh: v.tensor_tensor(
                    out=otmB[i % 2][:, jj * 448:jj * 448 + nh * 64].rearrange(
                        "p (h e) -> p h e", h=nh),
                    in0=psN[:, jj * 512:jj * 512 + nh * 65].rearrange(
                        "p (h e) -> p h e", h=nh)[:, :, 0:64],
                    in1=rr[:, 7 * jj:7 * jj + nh].unsqueeze(2).broadcast_to(
                        [128, nh, 64]),
                    op=ALU.mult),
                   reads=[f"psN{jj}", "rr"], writes=[f"otm{i % 2}"])
        k = i - 3
        if 0 <= k < NTC:
            ps, pks = (psA, ["psA0", "psA1"]) if k % 2 == 0 else (psB, ["psB0", "psB1"])
            for kc in range(8):
                for half in range(2):
                    sl = slice(half * 512, (half + 1) * 512)
                    T_(lambda t, k=k, kc=kc, sl=sl, ps=ps: t.matmul(
                        ps[:, sl], ofmB[k % 2][:, kc * 128:(kc + 1) * 128],
                        wob[:, kc, sl], start=(kc == 0), stop=False),
                       reads=[f"ofm{k % 2}", f"wo{kc}"], writes=[pks[half]],
                       cont=not (kc == 0 and half == 0))
            for half in range(2):
                sl = slice(half * 512, (half + 1) * 512)
                T_(lambda t, sl=sl, ps=ps: t.matmul(
                    ps[:, sl], ones[0:1, 0:128], b2rows[0:1, 0, sl],
                    start=False, stop=True),
                   reads=["ones", "bohi"], writes=[pks[half]], cont=True)
            p = k % 2
            B0 = 32 + 16 * p
            V_(lambda v, k=k, ps=ps, p=p: v.tensor_tensor(
                out=uB[p][:, :], in0=ps[:, :], in1=xldB[p][:, :], op=ALU.add),
               reads=pks + [f"xld{p}"],
               writes=[f"u{p}"] + (["kva_lo", "kva_hi", "t0"]
                                    if k == 0 else []))
            V_(lambda v, p=p, B0=B0: v.reduce_sum(out=stat[:, B0:B0 + 1],
                                                  in_=uB[p][:, :], axis=AXX),
               reads=[f"u{p}"], writes=[f"sa{p}0"])
            S_(lambda sc, p=p, B0=B0: sc.activation(
                out=wstC[:, :], in_=uB[p][:, :], func=AF.Square,
                accum_out=stat[:, B0 + 2:B0 + 3]),
               reads=[f"u{p}"], writes=["wstC", f"sa{p}2"] +
                                       (["xfm6"] if k == 0 else []))
        m = i - 4
        if 0 <= m < NTC:
            q = m % 2
            B0 = 32 + 16 * q
            q2 = (m + 1) % 2
            B2 = 32 + 16 * q2
            V_(lambda v, q=q, B0=B0: v.tensor_scalar(
                out=x1t[:, :], in0=uB[q][:, :],
                scalar1=stat[:, B0 + 4:B0 + 5],
                scalar2=stat[:, B0 + 12:B0 + 13],
                op0=ALU.subtract, op1=ALU.mult),
               reads=[f"u{q}", f"sa{q}m", f"sa{q}r"], writes=["x1t"])
            S_(lambda sc, B2=B2: sc.activation(out=wstC[:, :], in_=x1t[:, :],
                                               func=AF.Identity,
                                               accum_out=stat[:, B2 + 1:B2 + 2]),
               reads=["x1t"], writes=["wstC", f"sa{q2}1"])
            S_(lambda sc, B2=B2: sc.activation(out=wstC[:, :], in_=x1t[:, :],
                                               func=AF.Square,
                                               accum_out=stat[:, B2 + 3:B2 + 4]),
               reads=["x1t"], writes=["wstC", f"sa{q2}3"])
            Pd(lambda e, m=m: e.dma_start(out=x1_d[m * 128:(m + 1) * 128, :],
                                          in_=x1t[:, :]),
               reads=["x1t"], writes=[f"x1d{m}"])
        kc2 = i - 3
        if 0 <= kc2 <= NTC:
            pc = kc2 % 2
            Bc = 32 + 16 * pc
            racc = ([f"sa{pc}0", f"sa{pc}1"] if kc2 < NTC else [f"sa{pc}1"])
            S_(lambda sc, Bc=Bc: sc.activation(
                out=stat[:, Bc + 4:Bc + 6], in_=stat[:, Bc:Bc + 2],
                func=AF.Identity, scale=1.0 / C),
               reads=racc, writes=[f"sa{pc}m"])
            S_(lambda sc, Bc=Bc: sc.activation(
                out=stat[:, Bc + 6:Bc + 8], in_=stat[:, Bc + 4:Bc + 6],
                func=AF.Square),
               reads=[f"sa{pc}m"], writes=[f"sa{pc}q"])
            V_(lambda v, Bc=Bc: v.scalar_tensor_tensor(
                out=stat[:, Bc + 8:Bc + 10], in0=stat[:, Bc + 2:Bc + 4],
                scalar=1.0 / C, in1=stat[:, Bc + 6:Bc + 8],
                op0=ALU.mult, op1=ALU.subtract),
               reads=[f"sa{pc}q", f"sa{pc}2", f"sa{pc}3"], writes=[f"sa{pc}n"])
            S_(lambda sc, Bc=Bc: sc.activation(
                out=stat[:, Bc + 10:Bc + 12], in_=stat[:, Bc + 8:Bc + 10],
                func=AF.Sqrt, bias=epsc[:, 0:1]),
               reads=[f"sa{pc}n", "epsc"], writes=[f"sa{pc}s"])
            V_(lambda v, Bc=Bc: v.reciprocal(
                out=stat[:, Bc + 12:Bc + 14], in_=stat[:, Bc + 10:Bc + 12]),
               reads=[f"sa{pc}s"], writes=[f"sa{pc}r"])
            mp = kc2 - 1
            if mp >= 0:
                V_(lambda v, Bc=Bc, mp=mp: v.tensor_copy(
                    out=stat[:, 2 * mp:2 * mp + 1], in_=stat[:, Bc + 5:Bc + 6]),
                   reads=[f"sa{pc}m"], writes=[f"mr_{mp}"])
                V_(lambda v, Bc=Bc, mp=mp: v.tensor_copy(
                    out=stat[:, 2 * mp + 1:2 * mp + 2],
                    in_=stat[:, Bc + 13:Bc + 14]),
                   reads=[f"sa{pc}r"], writes=[f"mr_{mp}"])
        pump_comp(5)
        pump_dma(3, i)
        if i == 18:
            load_tc(0)
            load_tc(1)
        if i == 19:
            while dq_state["nc"] < len(dq_state["jobs"]):
                pump_dma(99)
                pump_comp(99)
            emit_h_tc(0)
            load_tc(2)
        if i == 20:
            emit_h_tc(1)
            load_tc(3)
        if i == 21:
            emit_h_tc(2)
            emit_h_tc(3)
    while dq_state["nc"] < len(dq_state["jobs"]):
        pump_dma(2)
        pump_comp(2)

    def emit_fc2_tail(s8):
        tcs = [s8 * 2, s8 * 2 + 1]
        for i, tc in enumerate(tcs):
            sl5 = tc % 5
            q = (s8 % 2) * 2 + i
            for fc in range(32):
                for half in range(2):
                    sl = slice(half * 512, (half + 1) * 512)
                    T_(lambda t, fc=fc, sl=sl, q=q: t.matmul(
                        psN[:, sl], y_fm[:, fc, q * 128:(q + 1) * 128],
                        w2b[:, fc, sl], start=(fc == 0), stop=False),
                       reads=[f"yf{fc}", f"w2b{fc}"], writes=[f"psN{half}"],
                       cont=not (fc == 0 and half == 0))
            for half in range(2):
                sl = slice(half * 512, (half + 1) * 512)
                T_(lambda t, sl=sl: t.matmul(
                    psN[:, sl], ones[0:1, 0:128], b2rows[0:1, 2, sl],
                    start=False, stop=True),
                   reads=["ones", "b2hi"], writes=[f"psN{half}"], cont=True)
            V_(lambda v, sl5=sl5: v.scalar_tensor_tensor(
                out=x1c[:, sl5, 0:512], in0=psN[:, 0:512], scalar=0.25,
                in1=x1c[:, sl5, 0:512], op0=ALU.mult, op1=ALU.add),
               reads=["psN0", f"x1c{sl5}"], writes=[f"x1c{sl5}"])
            V_(lambda v, sl5=sl5: v.scalar_tensor_tensor(
                out=x1c[:, sl5, 512:1024], in0=psN[:, 512:1024], scalar=0.25,
                in1=x1c[:, sl5, 512:1024], op0=ALU.mult, op1=ALU.add),
               reads=["psN1", f"x1c{sl5}"], writes=[f"x1c{sl5}"])
            V_(lambda v, i=i, sl5=sl5: v.reduce_sum(
                out=vb[:, i:i + 1], in_=x1c[:, sl5, :], axis=AXX),
               reads=[f"x1c{sl5}"], writes=[f"vb{i}"])
            S_(lambda sc, i=i, sl5=sl5: sc.activation(
                out=wstD[:, :], in_=x1c[:, sl5, :], func=AF.Square,
                accum_out=vb[:, 8 + i:9 + i]),
               reads=[f"x1c{sl5}"], writes=["wstD", f"vbs{i}"])
        for i in range(2):
            V_(lambda v, i=i: v.tensor_scalar_mul(
                out=vb[:, i:i + 1], in0=vb[:, i:i + 1], scalar1=1.0 / C),
               reads=[f"vb{i}"], writes=[f"vb{i}"])
            V_(lambda v, i=i: v.tensor_scalar_mul(
                out=vb[:, 8 + i:9 + i], in0=vb[:, 8 + i:9 + i], scalar1=1.0 / C),
               reads=[f"vbs{i}"], writes=[f"vbs{i}"])
            V_(lambda v, i=i: v.tensor_tensor(out=ss[:, 14:15],
                                              in0=vb[:, i:i + 1],
                                              in1=vb[:, i:i + 1], op=ALU.mult),
               reads=[f"vb{i}"], writes=["ss14"])
            V_(lambda v, i=i: v.tensor_tensor(out=vb[:, 8 + i:9 + i],
                                              in0=vb[:, 8 + i:9 + i],
                                              in1=ss[:, 14:15], op=ALU.subtract),
               reads=[f"vbs{i}", "ss14"], writes=[f"vbs{i}"])
        S_(lambda sc: sc.activation(out=vb[:, 12:14], in_=vb[:, 8:10],
                                    func=AF.Sqrt, bias=epsc[:, 1:2]),
           reads=["vbs0", "vbs1", "epsc"], writes=["sd4"])
        V_(lambda v: v.reciprocal(out=vb[:, 12:14], in_=vb[:, 12:14]),
           reads=["sd4"], writes=["sd4"])
        for i, tc in enumerate(tcs):
            sl5 = tc % 5
            V_(lambda v, i=i, sl5=sl5: v.tensor_scalar(
                out=x1c[:, sl5, :], in0=x1c[:, sl5, :],
                scalar1=vb[:, i:i + 1], scalar2=vb[:, 12 + i:13 + i],
                op0=ALU.subtract, op1=ALU.mult),
               reads=[f"x1c{sl5}", f"vb{i}", "sd4"], writes=[f"x1c{sl5}"])
            Yd(lambda e, tc=tc, sl5=sl5: e.dma_start(
                out=y_d[tc * 128:(tc + 1) * 128, :], in_=x1c[:, sl5, :]),
               reads=[f"x1c{sl5}"], writes=[f"yo{tc}"])

    for s4 in range(4):
        for nch in range(32):
            qsel = nch % 4
            pst = psA if qsel < 2 else psB
            bank = pst[:, (qsel % 2) * 512:(qsel % 2 + 1) * 512]
            bkey = ["psA0", "psA1", "psB0", "psB1"][qsel]
            nxt = ([["psA0", "psA1", "psB0", "psB1"][(qsel + 1) % 4]]
                   if nch % 2 == 0 else [])
            for kc in range(8):
                T_(lambda t, bank=bank, nch=nch, kc=kc: t.matmul(
                    bank, w1b[:, kc, nch * 128:(nch + 1) * 128],
                    h_fm[:, kc, :], start=(kc == 0), stop=(kc == 7)),
                   reads=([f"w1b{kc}"] + [f"h{kc}_{qq}" for qq in range(4)]
                          + (nxt if kc == 0 else [])),
                   writes=[bkey], cont=(kc != 0))
            S_(lambda sc, bank=bank, nch=nch: sc.activation(
                out=y_fm[:, nch, :], in_=bank, func=AF.Silu,
                bias=b1f[:, nch:nch + 1]),
               reads=[bkey, "b1f"],
               writes=[f"yf{nch}"] + (Y0AL if (s4 == 0 and nch == 0) else []))
        load_tc(4 * s4 + 4)
        emit_h_tc(4 * s4 + 4)
        emit_fc2_tail(2 * s4)
        load_tc(4 * s4 + 5)
        emit_h_tc(4 * s4 + 5)
        emit_fc2_tail(2 * s4 + 1)
        load_tc(4 * s4 + 6)
        emit_h_tc(4 * s4 + 6)
        load_tc(4 * s4 + 7)
        emit_h_tc(4 * s4 + 7)

    r.add("Y", lambda e: e.nop(), reads=[f"yo{tc}" for tc in range(NTC)],
          writes=[])

    ops = r.resolve()
    streams = {"T": [], "V": [], "S": [], "P": [], "Y": []}
    for op in ops:
        streams[op.eng].append(op)

    measured = {}

    def run_stream(eng_obj, lst):
        hw = {}
        for op in lst:
            for sem_key, val in op.waits:
                if hw.get(sem_key, 0) >= val:
                    continue
                eng_obj.wait_ge(sems[sem_key], val)
                hw[sem_key] = val
            if op.issue_wait is not None:
                skey, val = op.issue_wait
                if hw.get(skey, 0) < val:
                    eng_obj.wait_ge(sems[skey], val)
                    hw[skey] = val
            n0 = nc.n_instructions()
            ins = op.fn(eng_obj)
            if op.queue:
                measured[op.idx] = nc.n_instructions() - n0
            if op.inc:
                ins.then_inc(sems[op.qsem if op.queue else op.eng],
                             16 if op.queue else 1)

    with nc.Block() as blk:
        @blk.tensor
        def _(e):
            run_stream(e, streams["T"])

        @blk.vector
        def _(e):
            run_stream(e, streams["V"])

        @blk.scalar
        def _(e):
            run_stream(e, streams["S"])

        @blk.gpsimd
        def _(e):
            run_stream(e, streams["P"])

        @blk.sync
        def _(e):
            run_stream(e, streams["Y"])

    if dma_counts is None:
        return measured
    return nc


_NC_CACHE = [None]


def kernel(**inputs):
    inp = {k: np.asarray(v) for k, v in inputs.items()}
    x = inp["L0_attn_x"].astype(np.float32, copy=False)

    wkT = np.ascontiguousarray(inp["L0_attn_Wk"].T.astype(np.float32, copy=False))
    wvT = np.ascontiguousarray(inp["L0_attn_Wv"].T.astype(np.float32, copy=False))
    wqT = np.ascontiguousarray(inp["L0_attn_Wq"].T.astype(np.float32, copy=False))
    woT = np.ascontiguousarray(inp["L0_attn_Wo"].T.astype(np.float32, copy=False))
    brows = np.ascontiguousarray(np.stack([
        inp["L0_attn_bk"], inp["L0_attn_bv"], inp["L0_attn_bq"],
        inp["L0_attn_bo"], inp["L0_ffn_fc2_bias"]]).astype(np.float32))
    gamf = np.ascontiguousarray(
        inp["L0_ffn_ln_gamma"].astype(np.float32).reshape(8, 128).T)
    betf = np.ascontiguousarray(
        inp["L0_ffn_ln_beta"].astype(np.float32).reshape(8, 128).T)
    b1f = np.ascontiguousarray(
        inp["L0_ffn_fc1_bias"].astype(np.float32).reshape(32, 128).T)
    bqf = np.ascontiguousarray(
        inp["L0_attn_bq"].astype(np.float32).reshape(8, 128).T)
    q1 = np.ascontiguousarray(
        inp["L0_ffn_fc1_q"].astype(np.uint8, copy=False)
        .reshape(8, 128, 8, 512).transpose(0, 2, 1, 3).reshape(64 * 128, 512))
    s1 = np.ascontiguousarray(inp["L0_ffn_fc1_scales"].astype(np.float32, copy=False))
    q2 = np.ascontiguousarray(
        inp["L0_ffn_fc2_q"].astype(np.uint8, copy=False)
        .reshape(32, 128, 2, 512).transpose(0, 2, 1, 3).reshape(64 * 128, 512))
    s2 = np.ascontiguousarray(inp["L0_ffn_fc2_scales"].astype(np.float32, copy=False))

    if _NC_CACHE[0] is None:
        _NC_CACHE[0] = build_nc()
    nc = _NC_CACHE[0]

    in_maps = []
    for c in range(N_CORES):
        xc = np.ascontiguousarray(x[:, c * 512:(c + 1) * 512, :].reshape(M, C))
        in_maps.append(dict(
            x=xc, xT=np.ascontiguousarray(xc.T),
            wkT=wkT, wvT=wvT, wqT=wqT, woT=woT, brows=brows,
            gamf=gamf, betf=betf, b1f=b1f, bqf=bqf,
            q1=q1, s1=s1, q2=q2, s2=s2))

    res = run_bass_kernel_spmd(nc, in_maps, core_ids=list(range(N_CORES)))
    out = np.empty((B, T, C), np.float32)
    for c in range(N_CORES):
        out[:, c * 512:(c + 1) * 512, :] = res.results[c]["y"].reshape(B, 512, C)
    return out


# version 36 (final)
# speedup vs baseline: 1.5921x; 1.0525x over previous
import numpy as np
import concourse.bass as bass
import concourse.mybir as mybir
from concourse.bass_utils import run_bass_kernel_spmd

F32 = mybir.dt.float32
BF16 = mybir.dt.bfloat16
I32 = mybir.dt.int32
U8 = mybir.dt.uint8
AF = mybir.ActivationFunctionType
ALU = mybir.AluOpType
AXX = mybir.AxisListType.X

KB = 1024
N_CORES = 8
B, T, C = 4, 4096, 1024
H = 16
M = 2048
NTC = 16
F = 4096
EPS = 1e-5


def head_col(h):
    return (h // 7) * 512 + (h % 7) * 65


QROT = 8
SAFE_EDGES = True


class Op:
    __slots__ = ("eng", "fn", "reads", "writes", "queue", "idx", "inc",
                 "inc_val", "waits", "deps", "qsem", "issue_wait", "cont",
                 "guar")

    def __init__(self, eng, fn, reads, writes, queue, cont=False):
        self.eng = eng
        self.fn = fn
        self.reads = tuple(reads)
        self.writes = tuple(writes)
        self.queue = queue
        self.inc = False
        self.inc_val = None
        self.waits = []
        self.deps = []
        self.qsem = None
        self.issue_wait = None
        self.cont = cont
        self.guar = {}


class Rec:
    def __init__(self, dma_counts=None):
        self.ops = []
        self.dma_counts = dma_counts

    def add(self, eng, fn, reads=(), writes=(), dma=False, cont=False):
        self.ops.append(Op(eng, fn, reads, writes,
                           eng + "q" if dma else None, cont))

    def resolve(self):
        ops = self.ops
        for i, op in enumerate(ops):
            op.idx = i
        quse = {}
        scount = {}
        for op in ops:
            if op.queue is not None:
                op.inc = True
                i = quse.get(op.queue, 0)
                quse[op.queue] = i + 1
                skey = f"{op.queue}{i % QROT}"
                op.qsem = skey
                prev = scount.get(skey, 0)
                op.issue_wait = (skey, prev) if prev else None
                k = self.dma_counts.get(op.idx, 1) if self.dma_counts else 1
                scount[skey] = prev + 16 * k
                op.inc_val = scount[skey]
        writer = {}
        readers = {}
        for op in ops:
            deps = {}

            def dep_on(d, op=op, deps=deps):
                if d is None or d is op:
                    return
                src = d.qsem if d.queue else d.eng
                if d.queue is None and op.queue is not None and d.eng == op.eng:
                    return
                if (d.queue is None and op.queue is None
                        and d.eng == "T" and op.eng == "T" and op.cont):
                    return
                if (not SAFE_EDGES and d.queue is None and op.queue is None
                        and d.eng == op.eng and d.eng in ("V", "S")):
                    return
                cur = deps.get(src)
                if cur is None or d.idx > cur.idx:
                    deps[src] = d

            for b_ in op.reads:
                dep_on(writer.get(b_))
            for b_ in op.writes:
                dep_on(writer.get(b_))
                for rd in readers.get(b_, {}).values():
                    dep_on(rd)
            op.deps = list(deps.values())
            for d in op.deps:
                d.inc = True
            for b_ in op.reads:
                readers.setdefault(b_, {})[op.queue or op.eng] = op
            for b_ in op.writes:
                writer[b_] = op
                readers[b_] = {}
        count = {}
        for op in ops:
            if op.queue is None and op.inc:
                key = op.eng
                count[key] = count.get(key, 0) + 1
                op.inc_val = count[key]
        stream_cur = {}
        for op in ops:
            me = op.eng
            cur = stream_cur.setdefault(me, {})
            for d in sorted(op.deps, key=lambda d: -d.idx):
                key = d.qsem if d.queue else d.eng
                if cur.get(key, 0) < d.inc_val:
                    op.waits.append((key, d.inc_val))
                    cur[key] = d.inc_val
                for k2, v2 in d.guar.items():
                    if cur.get(k2, 0) < v2:
                        cur[k2] = v2
            g = dict(cur)
            if op.inc:
                k = op.qsem if op.queue else op.eng
                g[k] = max(g.get(k, 0), op.inc_val)
            op.guar = g
        return ops


def build_nc():
    counts = _build_inner(None)
    return _build_inner(counts)


def _build_inner(dma_counts):
    nc = bass.Bass("TRN2", target_bir_lowering=False, debug=False)

    x_d = nc.dram_tensor("x", [M, C], F32, kind="ExternalInput")
    xT_d = nc.dram_tensor("xT", [C, M], F32, kind="ExternalInput")
    wkT_d = nc.dram_tensor("wkT", [C, C], F32, kind="ExternalInput")
    wvT_d = nc.dram_tensor("wvT", [C, C], F32, kind="ExternalInput")
    wqT_d = nc.dram_tensor("wqT", [C, C], F32, kind="ExternalInput")
    woT_d = nc.dram_tensor("woT", [C, C], F32, kind="ExternalInput")
    brows_d = nc.dram_tensor("brows", [5, C], F32, kind="ExternalInput")
    gamf_d = nc.dram_tensor("gamf", [128, 8], F32, kind="ExternalInput")
    betf_d = nc.dram_tensor("betf", [128, 8], F32, kind="ExternalInput")
    b1f_d = nc.dram_tensor("b1f", [128, 32], F32, kind="ExternalInput")
    bqf_d = nc.dram_tensor("bqf", [128, 8], F32, kind="ExternalInput")
    q1_d = nc.dram_tensor("q1", [64 * 128, 512], U8, kind="ExternalInput")
    s1_d = nc.dram_tensor("s1", [8, F], F32, kind="ExternalInput")
    q2_d = nc.dram_tensor("q2", [64 * 128, 512], U8, kind="ExternalInput")
    s2_d = nc.dram_tensor("s2", [32, C], F32, kind="ExternalInput")
    y_d = nc.dram_tensor("y", [M, C], F32, kind="ExternalOutput")

    x1_d = nc.dram_tensor("x1spill", [M, C], F32)
    cc_in = nc.dram_tensor("cc_in", [64, 4160], F32)
    cc_out = nc.dram_tensor("cc_out", [64, 4160], F32)

    A = nc.alloc_sbuf_tensor_at
    O = 16 * KB
    ident = A("ident", [128, 128], BF16, offset=O + 1024)
    ones = A("ones", [1, 512], BF16, offset=O + 1280)
    gamf = A("gamf_s", [128, 8], F32, offset=O + 2304)
    betf = A("betf_s", [128, 8], F32, offset=O + 2336)
    b1f = A("b1f_s", [128, 32], F32, offset=O + 2368)
    stat = A("stat", [128, 64], F32, offset=O + 2496)
    bqf = A("bqf_s", [128, 8], F32, offset=O + 3072)
    ss = A("ss", [128, 16], F32, offset=O + 2752)
    vb = A("vb", [128, 16], F32, offset=O + 2816)
    epsc = A("epsc", [128, 4], F32, offset=O + 2880)
    den = A("den", [128, 16], F32, offset=O + 2944)
    rr = A("rr", [128, 16], F32, offset=O + 3008)
    b2rows = A("b2rows", [1, 4, 1024], BF16, offset=20 * KB)
    b1rows = A("b1rows", [1, 6, 1024], BF16, offset=28 * KB)
    qst = A("qst", [128, 4, 512], U8, offset=28 * KB)
    sbc = A("sbc", [128, 4, 512], F32, offset=30 * KB)
    u_t2 = A("u_t2", [128, 1024], F32, offset=152 * KB)
    xnt = A("xnt", [128, 1024], BF16, offset=38 * KB)
    wstC = A("wstC", [128, 1024], BF16, offset=129 * KB)
    wstD = A("wstD", [128, 1024], BF16, offset=156 * KB)
    xld2 = A("xld2", [128, 1024], F32, offset=131 * KB)
    ofm2 = A("ofm2", [128, 1024], BF16, offset=22 * KB)
    otm2 = A("otm2", [128, 1024], BF16, offset=26 * KB)
    t1q = A("t1q", [128, 1024], F32, offset=72 * KB)
    t2q = A("t2q", [128, 1024], F32, offset=76 * KB)
    stgB = A("stgB", [128, 2048], F32, offset=48 * KB)
    x1c = A("x1c", [128, 5, 1024], F32, offset=136 * KB)
    stgA = A("stgA", [128, 2048], F32, offset=40 * KB)
    stgC = A("stgC", [128, 2048], F32, offset=72 * KB)
    stgD = A("stgD", [128, 2048], F32, offset=80 * KB)
    wsg = A("wsg", [128, 4, 1024], F32, offset=159 * KB)
    brf = A("brf", [1, 1024], F32, offset=100 * KB)
    t1 = A("t1", [128, 1024], F32, offset=48 * KB)
    t2 = A("t2", [128, 1024], F32, offset=52 * KB)
    wkb = A("wkb", [128, 8, 1024], BF16, offset=56 * KB)
    wqb = A("wqb", [128, 8, 1024], BF16, offset=56 * KB)
    ktiles = A("ktiles", [128, 4, 1024], BF16, offset=72 * KB)
    v_aug = A("v_aug", [128, 4, 1040], BF16, offset=80 * KB)
    kv_stage = A("kv_stage", [64, 2, 1040], F32, offset=88 * KB + 256)
    w1b = A("w1b", [128, 8, 4096], BF16, offset=40 * KB)
    x_fm = A("x_fm", [128, 8, 2048], BF16, offset=104 * KB)
    kvr = A("kvr", [64, 4160], F32, offset=104 * KB)
    xld = A("xld", [128, 1024], F32, offset=104 * KB)
    ofm = A("ofm", [128, 1024], BF16, offset=108 * KB)
    otm = A("otm", [128, 1024], BF16, offset=110 * KB)
    u_t = A("u_t", [128, 1024], F32, offset=112 * KB)
    x1t = A("x1t", [128, 1024], F32, offset=116 * KB)
    kv_aug = A("kv_aug", [128, 4160], BF16, offset=120 * KB + 512)
    y_fm = A("y_fm", [128, 32, 512], BF16, offset=104 * KB)
    h_fm = A("h_fm", [128, 8, 512], BF16, offset=28 * KB)
    wvb = A("wvb", [128, 8, 1024], BF16, offset=136 * KB)
    wob = A("wob", [128, 8, 1024], BF16, offset=136 * KB)
    w2b = A("w2b", [128, 32, 1024], BF16, offset=159 * KB)
    q_fm = A("q_fm", [128, 8, 2048], BF16, offset=191 * KB)
    bk_bc = A("bk_bc", [128, 1024], F32, offset=191 * KB)
    bv_bc = A("bv_bc", [128, 1024], F32, offset=195 * KB)
    t0p = A("t0p", [128, 1024], F32, offset=199 * KB)
    t0q = A("t0q", [128, 1024], F32, offset=48 * KB)

    psA = nc.alloc_psum_tensor("psA", [128, 1024], F32)
    psB = nc.alloc_psum_tensor("psB", [128, 1024], F32)
    psN = nc.alloc_psum_tensor("psN", [128, 1536], F32)
    psT = nc.alloc_psum_tensor("psT", [128, 1024], BF16)

    sem_keys = ["T", "V", "S", "P", "Y"]
    for q in ["Yq", "Sq", "Pq"]:
        sem_keys += [f"{q}{i}" for i in range(QROT)]
    sems = {k: nc.alloc_semaphore("sem_" + k) for k in sem_keys}

    r = Rec(dma_counts)
    T_ = lambda fn, reads=(), writes=(), cont=False: r.add("T", fn, reads,
                                                           writes, cont=cont)
    V_ = lambda fn, reads=(), writes=(): r.add("V", fn, reads, writes)
    S_ = lambda fn, reads=(), writes=(): r.add("S", fn, reads, writes)
    P_ = lambda fn, reads=(), writes=(): r.add("P", fn, reads, writes)
    Yd = lambda fn, reads=(), writes=(): r.add("Y", fn, reads, writes, dma=True)
    Sd = lambda fn, reads=(), writes=(): r.add("S", fn, reads, writes, dma=True)
    Pd = lambda fn, reads=(), writes=(): r.add("P", fn, reads, writes, dma=True)

    XFM = [f"xfm{c}" for c in range(8)]
    VAK = [f"va{i}" for i in range(4)]
    KVS = ["kvs0", "kvs1"]
    WOK = [f"wo{c}" for c in range(8)]

    P_(lambda g: g.memset(ident[:, :], 0.0), writes=["ident", "ptok"])
    P_(lambda g: g.affine_select(out=ident[:, :], in_=ident[:, :],
                                 compare_op=ALU.not_equal, fill=1.0, base=0,
                                 pattern=[[-1, 128]], channel_multiplier=1),
       reads=["ident", "ptok"], writes=["ident", "ptok"])
    P_(lambda g: g.memset(ones[:, :], 1.0), writes=["ones", "ptok"])
    P_(lambda g: g.memset(epsc[:, 0:1], EPS), writes=["epsc", "ptok"])
    P_(lambda g: g.memset(epsc[:, 1:2], EPS / 4.0), writes=["epsc", "ptok"])
    P_(lambda g: g.memset(epsc[:, 2:3], -128.0), writes=["epsc", "ptok"])
    P_(lambda g: g.memset(stat[:, 32:64], 1024.0),
       writes=["ptok"] + [f"sa{p}{c}" for p in range(2) for c in "0123"])

    XSTG = [stgA, stgB, stgC, stgD]
    for c in range(8):
        stg, skey = XSTG[c % 4], f"xsg{c % 4}"
        Yd(lambda e, c=c, stg=stg: e.dma_start(
            out=stg[:, :], in_=xT_d[c * 128:(c + 1) * 128, :]),
           reads=[], writes=[skey])
        V_(lambda v, c=c, stg=stg: v.tensor_copy(out=x_fm[:, c, :],
                                                 in_=stg[:, :]),
           reads=[skey], writes=[f"xfm{c}"])
    def load_w(dram, dst, dstkey, alias=None):
        for c in range(8):
            slot = c % 4
            Pd(lambda e, c=c, slot=slot, dram=dram: e.dma_start(
                out=wsg[:, slot, :], in_=dram[c * 128:(c + 1) * 128, :]),
               writes=[f"wsg{slot}"])
            al = [f"{alias}{c}"] if alias else []
            S_(lambda sc, c=c, slot=slot, dst=dst: sc.copy(
                out=dst[:, c, :], in_=wsg[:, slot, :]),
               reads=[f"wsg{slot}"], writes=[f"{dstkey}{c}"] + al)

    load_w(wkT_d, wkb, "wk")
    load_w(wvT_d, wvb, "wv")
    P_(lambda g: g.memset(v_aug[:, :, :], 1.0),
       writes=VAK + ["ptok", "xsg3"])

    Yd(lambda e: e.dma_start(out=gamf[:, :], in_=gamf_d[:, :]), writes=["gamf"])
    Yd(lambda e: e.dma_start(out=betf[:, :], in_=betf_d[:, :]), writes=["betf"])
    Yd(lambda e: e.dma_start(out=b1f[:, :], in_=b1f_d[:, :]), writes=["b1f"])
    Yd(lambda e: e.dma_start(out=bqf[:, :], in_=bqf_d[:, :]), writes=["bqf"])

    for row, tgt, slot, nm in [(0, b1rows, 0, "bk"), (1, b1rows, 2, "bv"),
                               (3, b2rows, 0, "bo"),
                               (4, b2rows, 2, "b2")]:
        hi = nm + "hi"
        Yd(lambda e, row=row: e.dma_start(out=brf[:, :],
                                          in_=brows_d[row:row + 1, :]),
           writes=["brf"])
        S_(lambda sc, tgt=tgt, slot=slot: sc.copy(
            out=tgt[:, slot, :], in_=brf[:, :]),
           reads=["brf"], writes=[hi])

    for half in range(2):
        sl = slice(half * 512, (half + 1) * 512)
        T_(lambda t, sl=sl: t.matmul(psA[:, sl], ones[0:1, 0:128],
                                     b1rows[0:1, 0, sl], start=True, stop=True),
           reads=["ones", "bkhi"], writes=["psA0" if half == 0 else "psA1"])
        T_(lambda t, sl=sl: t.matmul(psB[:, sl], ones[0:1, 0:128],
                                     b1rows[0:1, 2, sl], start=True, stop=True),
           reads=["ones", "bvhi"], writes=["psB0" if half == 0 else "psB1"])
    V_(lambda v: v.tensor_copy(out=bk_bc[:, :], in_=psA[:, :]),
       reads=["psA0", "psA1"], writes=["bkbc"])
    V_(lambda v: v.tensor_copy(out=bv_bc[:, :], in_=psB[:, :]),
       reads=["psB0", "psB1"], writes=["bvbc"])

    for b in range(B):
        for i in range(4):
            tc = 4 * b + i
            for half in range(2):
                pk = f"psA{half}"
                sl = slice(half * 512, (half + 1) * 512)
                for kc in range(8):
                    T_(lambda t, kc=kc, sl=sl, tc=tc: t.matmul(
                        psA[:, sl], x_fm[:, kc, tc * 128:(tc + 1) * 128],
                        wkb[:, kc, sl], start=(kc == 0), stop=(kc == 7)),
                       reads=[f"xfm{kc}", f"wk{kc}"], writes=[pk],
                       cont=(kc != 0))
            kal = ["xsg1"] if tc == 0 else []
            V_(lambda v: v.tensor_tensor(out=t0p[:, :], in0=psA[:, :],
                                         in1=bk_bc[:, :], op=ALU.add),
               reads=["psA0", "psA1", "bkbc"], writes=["t0"])
            S_(lambda sc: sc.activation(out=t1[:, :], in_=t0p[:, :],
                                        func=AF.Relu, scale=-1.0),
               reads=["t0"], writes=["t1"] + kal)
            S_(lambda sc: sc.activation(out=t2[:, :], in_=t1[:, :],
                                        func=AF.Exp, scale=-1.0),
               reads=["t1"], writes=["t2"] + kal)
            V_(lambda v, i=i: v.scalar_tensor_tensor(
                out=ktiles[:, i, :], in0=t0p[:, :], scalar=0.0,
                in1=t2[:, :], op0=ALU.max, op1=ALU.add),
               reads=["t0", "t2"],
               writes=[f"kt{i}"] + (["xsg2"] if tc < 4 else []))
            for half in range(2):
                pk = f"psB{half}"
                sl = slice(half * 512, (half + 1) * 512)
                for kc in range(8):
                    T_(lambda t, kc=kc, sl=sl, tc=tc: t.matmul(
                        psB[:, sl], x_fm[:, kc, tc * 128:(tc + 1) * 128],
                        wvb[:, kc, sl], start=(kc == 0), stop=(kc == 7)),
                       reads=[f"xfm{kc}", f"wv{kc}"], writes=[pk],
                       cont=(kc != 0))
            V_(lambda v: v.tensor_tensor(out=t0p[:, :], in0=psB[:, :],
                                         in1=bv_bc[:, :], op=ALU.add),
               reads=["psB0", "psB1", "bvbc"], writes=["t0"])
            S_(lambda sc, i=i: sc.activation(
                out=v_aug[:, i, :].rearrange("p (h e) -> p h e", h=16)[:, :, 0:64],
                in_=t0p[:, :].rearrange("p (h e) -> p h e", h=16),
                func=AF.Identity),
               reads=["t0"], writes=[f"va{i}"])
        for h in range(H):
            col = head_col(h)
            for i in range(4):
                T_(lambda t, h=h, i=i, col=col: t.matmul(
                    psN[0:64, col:col + 65],
                    ktiles[:, i, h * 64:(h + 1) * 64],
                    v_aug[:, i, h * 65:(h + 1) * 65],
                    start=(i == 0), stop=(i == 3)),
                   reads=[f"kt{i}", f"va{i}"], writes=[f"psN{h // 7}"],
                   cont=(i != 0))
        buf = b % 2
        for j, nh in [(0, 7), (1, 7), (2, 2)]:
            V_(lambda v, buf=buf, j=j, nh=nh: v.tensor_copy(
                out=kv_stage[:, buf, :].rearrange(
                    "p (h e) -> p h e", h=16)[:, 7 * j:7 * j + nh, :],
                in_=psN[0:64, j * 512:j * 512 + nh * 65].rearrange(
                    "p (h e) -> p h e", h=nh)),
               reads=[f"psN{j}"], writes=[f"kvs{buf}"])
        Yd(lambda e, b=b, buf=buf: e.dma_start(
            out=cc_in[:, b * 1040:(b + 1) * 1040],
            in_=kv_stage[:, buf, :]),
           reads=[f"kvs{buf}"], writes=[f"ccin{b}"])

    KTK = [f"kt{i}" for i in range(4)]
    W1AL = {0: ["xsg0"], 1: ["t1", "t2", "xsg1", "t0q"],
            2: ["wq0", "wq1", "wq2", "wq3"], 3: ["wq4", "wq5", "wq6", "wq7"],
            4: KTK + ["t1q", "t2q", "xsg2"], 5: VAK + ["xsg3"],
            6: VAK + KVS, 7: KVS}

    def dq_aps(job):
        kind, ch, s = job
        if kind == "w1":
            qsrc = q1_d[(ch * 8 + s) * 128:(ch * 8 + s + 1) * 128, :]
            ssrc = s1_d[ch:ch + 1, s * 512:(s + 1) * 512]
            dst = w1b[:, ch, s * 512:(s + 1) * 512]
            dkey = f"w1b{ch}"
            al = W1AL.get(ch, []) if s == 0 else []
        else:
            qsrc = q2_d[(ch * 2 + s) * 128:(ch * 2 + s + 1) * 128, :]
            ssrc = s2_d[ch:ch + 1, s * 512:(s + 1) * 512]
            dst = w2b[:, ch, s * 512:(s + 1) * 512]
            dkey = f"w2b{ch}"
            al = ([f"qf{(ch - 16) // 2}_{(ch - 16) % 2}"]
                  if (ch >= 16 and s == 0) else
                  [f"wsg{ch // 2}"] if (ch < 8 and s == 0) else [])
        return qsrc, ssrc, dst, dkey, al

    def emit_dq_dma(job, slot, first=False):
        qsrc, ssrc, dst, dkey, al = dq_aps(job)
        qk, sk = f"qst{slot}", f"sbc{slot}"
        qal = ["bkhi"] if first else []
        sal = ["bvhi"] if first else []
        Yd(lambda e, qsrc=qsrc, slot=slot: e.dma_start(
            out=qst[:, slot, :], in_=qsrc), writes=[qk] + qal)
        Yd(lambda e, ssrc=ssrc, slot=slot: e.dma_start(
            out=sbc[:, slot, :], in_=ssrc.partition_broadcast(128)),
           writes=[sk] + sal)

    def emit_dq_comp(job, slot):
        qsrc, ssrc, dst, dkey, al = dq_aps(job)
        qk, sk = f"qst{slot}", f"sbc{slot}"
        V_(lambda v, dst=dst, slot=slot: v.scalar_tensor_tensor(
            out=dst, in0=qst[:, slot, :], scalar=128.0,
            in1=sbc[:, slot, :], op0=ALU.subtract, op1=ALU.mult),
           reads=[qk, sk], writes=[dkey] + al)

    dq_p2 = ([("w1", kc, s) for kc in (5, 6, 7, 0) for s in range(8)] +
             [("w2", fc, s) for fc in range(8, 16) for s in range(2)])
    dq_c = ([("w1", kc, s) for kc in (1, 2, 3, 4) for s in range(8)] +
            [("w2", fc, s) for fc in range(8) for s in range(2)] +
            [("w2", fc, s) for fc in range(16, 32, 2) for s in range(2)] +
            [("w2", fc, s) for fc in range(17, 32, 2) for s in range(2)])
    dq_gates = ([0] * 48 + [9] * 16 + [16] * 16)
    dq_state = {"nd": 0, "nc": 0, "tot0": 0, "jobs": dq_p2,
                "gates": [0] * len(dq_p2)}

    def dq_switch(jobs, gates=None):
        st = dq_state
        assert st["nd"] == st["nc"] == len(st["jobs"])
        st["tot0"] += len(st["jobs"])
        st["jobs"] = jobs
        st["gates"] = gates if gates is not None else [0] * len(jobs)
        st["nd"] = st["nc"] = 0

    def pump_dma(k, it=10 ** 9):
        st = dq_state
        while (k > 0 and st["nd"] < len(st["jobs"])
               and st["nd"] < st["nc"] + 4
               and st["gates"][st["nd"]] <= it):
            j = st["nd"]
            emit_dq_dma(st["jobs"][j], (st["tot0"] + j) % 4,
                        first=(st["tot0"] + j == 0))
            st["nd"] += 1
            k -= 1

    def pump_comp(k):
        st = dq_state
        while k > 0 and st["nc"] < min(st["nd"], len(st["jobs"])):
            j = st["nc"]
            emit_dq_comp(st["jobs"][j], (st["tot0"] + j) % 4)
            st["nc"] += 1
            k -= 1

    def pump(k):
        pump_dma(k)
        pump_comp(k)

    load_w(wqT_d, wqb, "wq", alias="wk")
    load_w(woT_d, wob, "wo", alias="wv")

    P_(lambda g: g.collective_compute(
        "AllReduce", ALU.add, replica_groups=[list(range(N_CORES))],
        ins=[cc_in[:, :]], outs=[cc_out[:, :]]),
       reads=[f"ccin{b_}" for b_ in range(B)] + ["ptok"],
       writes=["ccout", "ptok"])

    for nch in range(8):
        for strip in range(2):
            ps, pks = (psA, ["psA0", "psA1"]) if (nch * 2 + strip) % 2 == 0 \
                else (psB, ["psB0", "psB1"])
            for half in range(2):
                sl = slice(half * 512, (half + 1) * 512)
                t0 = strip * 1024 + half * 512
                for kc in range(8):
                    T_(lambda t, ps=ps, nch=nch, kc=kc, sl=sl, t0=t0: t.matmul(
                        ps[:, sl], wqb[:, kc, nch * 128:(nch + 1) * 128],
                        x_fm[:, kc, t0:t0 + 512],
                        start=(kc == 0), stop=(kc == 7)),
                       reads=[f"wq{kc}", f"xfm{kc}"], writes=[pks[half]],
                       cont=(kc != 0))
            qa1 = ["kt0", "kt1"] if (nch == 0 and strip == 0) else []
            qa2 = ["kt2", "kt3"] if (nch == 0 and strip == 0) else []
            qa0 = (["t1", "xsg1"] if (nch == 0 and strip == 0) else [])
            S_(lambda sc, ps=ps, nch=nch: sc.activation(
                out=t0q[:, :], in_=ps[:, :], func=AF.Identity,
                bias=bqf[:, nch:nch + 1]),
               reads=pks + ["bqf"], writes=["t0q"] + qa0)
            S_(lambda sc: sc.activation(out=t1q[:, :], in_=t0q[:, :],
                                        func=AF.Relu, scale=-1.0),
               reads=["t0q"], writes=["t1q"] + qa1)
            S_(lambda sc: sc.activation(out=t2q[:, :], in_=t1q[:, :],
                                        func=AF.Exp, scale=-1.0),
               reads=["t1q"], writes=["t2q"] + qa2)
            V_(lambda v, nch=nch, strip=strip: v.scalar_tensor_tensor(
                out=q_fm[:, nch, strip * 1024:(strip + 1) * 1024],
                in0=t0q[:, :], scalar=0.0, in1=t2q[:, :],
                op0=ALU.max, op1=ALU.add),
               reads=["t0q", "t2q"],
               writes=[f"qf{nch}_{strip}"] +
                      (["bkbc", "bvbc"] if (nch == 0 and strip == 0) else []) +
                      (["t0"] if (nch == 2 and strip == 0) else []))
            pump_dma(2)
            pump_comp(2)

    while dq_state["nc"] < len(dq_state["jobs"]):
        pump_dma(2)
        pump_comp(2)
    dq_switch(dq_c, dq_gates)

    Yd(lambda e: e.dma_start(out=kvr[:, :], in_=cc_out[:, :]),
       reads=["ccout"], writes=["kvr"] + XFM)
    V_(lambda v: v.tensor_copy(out=kv_aug[0:64, :], in_=kvr[:, :]),
       reads=["kvr"], writes=["kva_lo"])
    Pd(lambda e: e.dma_start(out=kv_aug[64:128, :], in_=kv_aug[0:64, :]),
       reads=["kva_lo"], writes=["kva_hi"])

    QSB = [f"qst{j}" for j in range(4)] + [f"sbc{j}" for j in range(3)]
    Y0AL = ["xld0", "ofm0", "otm0", "u0", "x1t", "kva_lo", "kva_hi",
            "wstC", "xld1"]
    X1AL = {0: ["wo0", "wo1"], 1: ["wo2", "wo3"], 2: ["wo4", "wo5"],
            3: ["wo6", "wo7"], 4: ["u1"]}

    def load_tc(tc):
        if not 0 <= tc < NTC:
            return
        sl5 = tc % 5
        x1al = X1AL.get(sl5, []) if tc < 5 else []
        Pd(lambda e, tc=tc, sl5=sl5: e.dma_start(
            out=x1c[:, sl5, :], in_=x1_d[tc * 128:(tc + 1) * 128, :]),
           reads=[f"x1d{tc}"], writes=[f"x1c{sl5}"] + x1al)

    def emit_h_tc(tc):
        if not 0 <= tc < NTC:
            return
        sl5 = tc % 5
        q = tc % 4
        pos = q * 128
        V_(lambda v, tc=tc, sl5=sl5: v.tensor_scalar(
            out=xnt[:, :], in0=x1c[:, sl5, :],
            scalar1=stat[:, 2 * tc:2 * tc + 1],
            scalar2=stat[:, 2 * tc + 1:2 * tc + 2],
            op0=ALU.subtract, op1=ALU.mult),
           reads=[f"x1c{sl5}", f"mr_{tc}"], writes=["xnt"])
        for kc in range(8):
            T_(lambda t, kc=kc: t.transpose(
                psT[:, kc * 128:(kc + 1) * 128],
                xnt[:, kc * 128:(kc + 1) * 128], ident[:, :]),
               reads=["xnt", "ident"], writes=["psT"])
        hal = QSB if tc == 0 else []
        for kc in range(8):
            V_(lambda v, kc=kc, pos=pos, q=q: v.tensor_scalar(
                out=h_fm[:, kc, pos:pos + 128],
                in0=psT[:, kc * 128:(kc + 1) * 128],
                scalar1=gamf[:, kc:kc + 1], scalar2=betf[:, kc:kc + 1],
                op0=ALU.mult, op1=ALU.add),
               reads=["psT", "gamf", "betf"],
               writes=[f"h{kc}_{q}"] + (hal if kc == 0 else []))


    uB = [u_t, u_t2]
    xldB = [xld, xld2]
    otmB = [otm, otm2]
    ofmB = [ofm, ofm2]

    for i in range(NTC + 6):
        pump_dma(5, i)
        jx = i - 2
        if 0 <= jx < NTC:
            xal = (["kvr"] if jx == 0 else []) + (["otm2"] if jx == 1 else [])
            Yd(lambda e, jx=jx: e.dma_start(out=xldB[jx % 2][:, :],
                                            in_=x_d[jx * 128:(jx + 1) * 128, :]),
               reads=[], writes=[f"xld{jx % 2}"] + xal)
        if i < NTC:
            b = i // 4
            for h in range(H):
                col = head_col(h)
                base = (h % 2) * 64
                T_(lambda t, i=i, h=h, col=col, base=base, b=b: t.matmul(
                    psN[:, col:col + 65],
                    q_fm[base:base + 64, h // 2, i * 128:(i + 1) * 128],
                    kv_aug[base:base + 64, b * 1040 + h * 65:b * 1040 + (h + 1) * 65],
                    start=True, stop=True),
                   reads=[f"qf{h // 2}_{i // 8}", "kva_lo", "kva_hi"],
                   writes=[f"psN{h // 7}"])
        j = i - 2
        if 0 <= j < NTC:
            for kc in range(8):
                T_(lambda t, j=j, kc=kc: t.transpose(
                    psT[:, kc * 128:(kc + 1) * 128],
                    otmB[j % 2][:, kc * 128:(kc + 1) * 128], ident[:, :]),
                   reads=[f"otm{j % 2}", "ident"], writes=["psT"])
            eal = ["ofm2"] if j == 1 else []
            S_(lambda sc, j=j: sc.copy(out=ofmB[j % 2][:, :], in_=psT[:, :]),
               reads=["psT"], writes=[f"ofm{j % 2}"] + eal)
        if i < NTC:
            for jj, nh in [(0, 7), (1, 7), (2, 2)]:
                V_(lambda v, jj=jj, nh=nh: v.tensor_copy(
                    out=den[:, 7 * jj:7 * jj + nh],
                    in_=psN[:, jj * 512:jj * 512 + nh * 65].rearrange(
                        "p (h e) -> p h e", h=nh)[:, :, 64:65].squeeze(2)),
                   reads=[f"psN{jj}"], writes=["den"])
            V_(lambda v: v.tensor_scalar_max(out=den[:, :], in0=den[:, :],
                                             scalar1=1e-6),
               reads=["den"], writes=["den"])
            V_(lambda v: v.reciprocal(out=rr[:, :], in_=den[:, :]),
               reads=["den"], writes=["rr"])
            for jj, nh in [(0, 7), (1, 7), (2, 2)]:
                V_(lambda v, i=i, jj=jj, nh=nh: v.tensor_tensor(
                    out=otmB[i % 2][:, jj * 448:jj * 448 + nh * 64].rearrange(
                        "p (h e) -> p h e", h=nh),
                    in0=psN[:, jj * 512:jj * 512 + nh * 65].rearrange(
                        "p (h e) -> p h e", h=nh)[:, :, 0:64],
                    in1=rr[:, 7 * jj:7 * jj + nh].unsqueeze(2).broadcast_to(
                        [128, nh, 64]),
                    op=ALU.mult),
                   reads=[f"psN{jj}", "rr"], writes=[f"otm{i % 2}"])
        k = i - 3
        if 0 <= k < NTC:
            ps, pks = (psA, ["psA0", "psA1"]) if k % 2 == 0 else (psB, ["psB0", "psB1"])
            for half in range(2):
                sl = slice(half * 512, (half + 1) * 512)
                for kc in range(8):
                    T_(lambda t, k=k, kc=kc, sl=sl, ps=ps: t.matmul(
                        ps[:, sl], ofmB[k % 2][:, kc * 128:(kc + 1) * 128],
                        wob[:, kc, sl], start=(kc == 0), stop=False),
                       reads=[f"ofm{k % 2}", f"wo{kc}"], writes=[pks[half]],
                       cont=(kc != 0))
                T_(lambda t, sl=sl, ps=ps: t.matmul(
                    ps[:, sl], ones[0:1, 0:128], b2rows[0:1, 0, sl],
                    start=False, stop=True),
                   reads=["ones", "bohi"], writes=[pks[half]], cont=True)
            p = k % 2
            B0 = 32 + 16 * p
            V_(lambda v, k=k, ps=ps, p=p: v.tensor_tensor(
                out=uB[p][:, :], in0=ps[:, :], in1=xldB[p][:, :], op=ALU.add),
               reads=pks + [f"xld{p}"],
               writes=[f"u{p}"] + (["kva_lo", "kva_hi", "t0"]
                                    if k == 0 else []))
            V_(lambda v, p=p, B0=B0: v.reduce_sum(out=stat[:, B0:B0 + 1],
                                                  in_=uB[p][:, :], axis=AXX),
               reads=[f"u{p}"], writes=[f"sa{p}0"])
            S_(lambda sc, p=p, B0=B0: sc.activation(
                out=wstC[:, :], in_=uB[p][:, :], func=AF.Square,
                accum_out=stat[:, B0 + 2:B0 + 3]),
               reads=[f"u{p}"], writes=["wstC", f"sa{p}2"] +
                                       (["xfm6"] if k == 0 else []))
        m = i - 4
        if 0 <= m < NTC:
            q = m % 2
            B0 = 32 + 16 * q
            q2 = (m + 1) % 2
            B2 = 32 + 16 * q2
            V_(lambda v, q=q, B0=B0: v.tensor_scalar(
                out=x1t[:, :], in0=uB[q][:, :],
                scalar1=stat[:, B0 + 4:B0 + 5],
                scalar2=stat[:, B0 + 12:B0 + 13],
                op0=ALU.subtract, op1=ALU.mult),
               reads=[f"u{q}", f"sa{q}m", f"sa{q}r"], writes=["x1t"])
            S_(lambda sc, B2=B2: sc.activation(out=wstC[:, :], in_=x1t[:, :],
                                               func=AF.Identity,
                                               accum_out=stat[:, B2 + 1:B2 + 2]),
               reads=["x1t"], writes=["wstC", f"sa{q2}1"])
            S_(lambda sc, B2=B2: sc.activation(out=wstC[:, :], in_=x1t[:, :],
                                               func=AF.Square,
                                               accum_out=stat[:, B2 + 3:B2 + 4]),
               reads=["x1t"], writes=["wstC", f"sa{q2}3"])
            Pd(lambda e, m=m: e.dma_start(out=x1_d[m * 128:(m + 1) * 128, :],
                                          in_=x1t[:, :]),
               reads=["x1t"], writes=[f"x1d{m}"])
        kc2 = i - 3
        if 0 <= kc2 <= NTC:
            pc = kc2 % 2
            Bc = 32 + 16 * pc
            racc = ([f"sa{pc}0", f"sa{pc}1"] if kc2 < NTC else [f"sa{pc}1"])
            S_(lambda sc, Bc=Bc: sc.activation(
                out=stat[:, Bc + 4:Bc + 6], in_=stat[:, Bc:Bc + 2],
                func=AF.Identity, scale=1.0 / C),
               reads=racc, writes=[f"sa{pc}m"])
            S_(lambda sc, Bc=Bc: sc.activation(
                out=stat[:, Bc + 6:Bc + 8], in_=stat[:, Bc + 4:Bc + 6],
                func=AF.Square),
               reads=[f"sa{pc}m"], writes=[f"sa{pc}q"])
            V_(lambda v, Bc=Bc: v.scalar_tensor_tensor(
                out=stat[:, Bc + 8:Bc + 10], in0=stat[:, Bc + 2:Bc + 4],
                scalar=1.0 / C, in1=stat[:, Bc + 6:Bc + 8],
                op0=ALU.mult, op1=ALU.subtract),
               reads=[f"sa{pc}q", f"sa{pc}2", f"sa{pc}3"], writes=[f"sa{pc}n"])
            S_(lambda sc, Bc=Bc: sc.activation(
                out=stat[:, Bc + 10:Bc + 12], in_=stat[:, Bc + 8:Bc + 10],
                func=AF.Sqrt, bias=epsc[:, 0:1]),
               reads=[f"sa{pc}n", "epsc"], writes=[f"sa{pc}s"])
            V_(lambda v, Bc=Bc: v.reciprocal(
                out=stat[:, Bc + 12:Bc + 14], in_=stat[:, Bc + 10:Bc + 12]),
               reads=[f"sa{pc}s"], writes=[f"sa{pc}r"])
            mp = kc2 - 1
            if mp >= 0:
                V_(lambda v, Bc=Bc, mp=mp: v.tensor_copy(
                    out=stat[:, 2 * mp:2 * mp + 1], in_=stat[:, Bc + 5:Bc + 6]),
                   reads=[f"sa{pc}m"], writes=[f"mr_{mp}"])
                V_(lambda v, Bc=Bc, mp=mp: v.tensor_copy(
                    out=stat[:, 2 * mp + 1:2 * mp + 2],
                    in_=stat[:, Bc + 13:Bc + 14]),
                   reads=[f"sa{pc}r"], writes=[f"mr_{mp}"])
        pump_comp(5)
        pump_dma(3, i)
        if i == 18:
            load_tc(0)
            load_tc(1)
        if i == 19:
            while dq_state["nc"] < len(dq_state["jobs"]):
                pump_dma(99)
                pump_comp(99)
            emit_h_tc(0)
            load_tc(2)
        if i == 20:
            emit_h_tc(1)
            load_tc(3)
        if i == 21:
            emit_h_tc(2)
            emit_h_tc(3)
    while dq_state["nc"] < len(dq_state["jobs"]):
        pump_dma(2)
        pump_comp(2)

    def emit_fc2_tail(s8):
        tcs = [s8 * 2, s8 * 2 + 1]
        for i, tc in enumerate(tcs):
            sl5 = tc % 5
            q = (s8 % 2) * 2 + i
            for half in range(2):
                nk = f"psN{half}"
                sl = slice(half * 512, (half + 1) * 512)
                for fc in range(32):
                    T_(lambda t, fc=fc, sl=sl, q=q: t.matmul(
                        psN[:, sl], y_fm[:, fc, q * 128:(q + 1) * 128],
                        w2b[:, fc, sl], start=(fc == 0), stop=False),
                       reads=[f"yf{fc}", f"w2b{fc}"], writes=[nk],
                       cont=(fc != 0))
                T_(lambda t, sl=sl: t.matmul(
                    psN[:, sl], ones[0:1, 0:128], b2rows[0:1, 2, sl],
                    start=False, stop=True),
                   reads=["ones", "b2hi"], writes=[nk], cont=True)
            V_(lambda v, sl5=sl5: v.scalar_tensor_tensor(
                out=x1c[:, sl5, 0:512], in0=psN[:, 0:512], scalar=0.25,
                in1=x1c[:, sl5, 0:512], op0=ALU.mult, op1=ALU.add),
               reads=["psN0", f"x1c{sl5}"], writes=[f"x1c{sl5}"])
            V_(lambda v, sl5=sl5: v.scalar_tensor_tensor(
                out=x1c[:, sl5, 512:1024], in0=psN[:, 512:1024], scalar=0.25,
                in1=x1c[:, sl5, 512:1024], op0=ALU.mult, op1=ALU.add),
               reads=["psN1", f"x1c{sl5}"], writes=[f"x1c{sl5}"])
            V_(lambda v, i=i, sl5=sl5: v.reduce_sum(
                out=vb[:, i:i + 1], in_=x1c[:, sl5, :], axis=AXX),
               reads=[f"x1c{sl5}"], writes=[f"vb{i}"])
            S_(lambda sc, i=i, sl5=sl5: sc.activation(
                out=wstD[:, :], in_=x1c[:, sl5, :], func=AF.Square,
                accum_out=vb[:, 8 + i:9 + i]),
               reads=[f"x1c{sl5}"], writes=["wstD", f"vbs{i}"])
        for i in range(2):
            V_(lambda v, i=i: v.tensor_scalar_mul(
                out=vb[:, i:i + 1], in0=vb[:, i:i + 1], scalar1=1.0 / C),
               reads=[f"vb{i}"], writes=[f"vb{i}"])
            V_(lambda v, i=i: v.tensor_scalar_mul(
                out=vb[:, 8 + i:9 + i], in0=vb[:, 8 + i:9 + i], scalar1=1.0 / C),
               reads=[f"vbs{i}"], writes=[f"vbs{i}"])
            V_(lambda v, i=i: v.tensor_tensor(out=ss[:, 14:15],
                                              in0=vb[:, i:i + 1],
                                              in1=vb[:, i:i + 1], op=ALU.mult),
               reads=[f"vb{i}"], writes=["ss14"])
            V_(lambda v, i=i: v.tensor_tensor(out=vb[:, 8 + i:9 + i],
                                              in0=vb[:, 8 + i:9 + i],
                                              in1=ss[:, 14:15], op=ALU.subtract),
               reads=[f"vbs{i}", "ss14"], writes=[f"vbs{i}"])
        S_(lambda sc: sc.activation(out=vb[:, 12:14], in_=vb[:, 8:10],
                                    func=AF.Sqrt, bias=epsc[:, 1:2]),
           reads=["vbs0", "vbs1", "epsc"], writes=["sd4"])
        V_(lambda v: v.reciprocal(out=vb[:, 12:14], in_=vb[:, 12:14]),
           reads=["sd4"], writes=["sd4"])
        for i, tc in enumerate(tcs):
            sl5 = tc % 5
            V_(lambda v, i=i, sl5=sl5: v.tensor_scalar(
                out=x1c[:, sl5, :], in0=x1c[:, sl5, :],
                scalar1=vb[:, i:i + 1], scalar2=vb[:, 12 + i:13 + i],
                op0=ALU.subtract, op1=ALU.mult),
               reads=[f"x1c{sl5}", f"vb{i}", "sd4"], writes=[f"x1c{sl5}"])
            Yd(lambda e, tc=tc, sl5=sl5: e.dma_start(
                out=y_d[tc * 128:(tc + 1) * 128, :], in_=x1c[:, sl5, :]),
               reads=[f"x1c{sl5}"], writes=[f"yo{tc}"])

    for s4 in range(4):
        for nch in range(32):
            qsel = nch % 4
            pst = psA if qsel < 2 else psB
            bank = pst[:, (qsel % 2) * 512:(qsel % 2 + 1) * 512]
            bkey = ["psA0", "psA1", "psB0", "psB1"][qsel]
            nxt = ([["psA0", "psA1", "psB0", "psB1"][(qsel + 1) % 4]]
                   if nch % 2 == 0 else [])
            for kc in range(8):
                T_(lambda t, bank=bank, nch=nch, kc=kc: t.matmul(
                    bank, w1b[:, kc, nch * 128:(nch + 1) * 128],
                    h_fm[:, kc, :], start=(kc == 0), stop=(kc == 7)),
                   reads=([f"w1b{kc}"] + [f"h{kc}_{qq}" for qq in range(4)]
                          + (nxt if kc == 0 else [])),
                   writes=[bkey], cont=(kc != 0))
            S_(lambda sc, bank=bank, nch=nch: sc.activation(
                out=y_fm[:, nch, :], in_=bank, func=AF.Silu,
                bias=b1f[:, nch:nch + 1]),
               reads=[bkey, "b1f"],
               writes=[f"yf{nch}"] + (Y0AL if (s4 == 0 and nch == 0) else []))
        load_tc(4 * s4 + 4)
        emit_h_tc(4 * s4 + 4)
        emit_fc2_tail(2 * s4)
        load_tc(4 * s4 + 5)
        emit_h_tc(4 * s4 + 5)
        emit_fc2_tail(2 * s4 + 1)
        load_tc(4 * s4 + 6)
        emit_h_tc(4 * s4 + 6)
        load_tc(4 * s4 + 7)
        emit_h_tc(4 * s4 + 7)

    r.add("Y", lambda e: e.nop(), reads=[f"yo{tc}" for tc in range(NTC)],
          writes=[])

    ops = r.resolve()
    streams = {"T": [], "V": [], "S": [], "P": [], "Y": []}
    for op in ops:
        streams[op.eng].append(op)

    measured = {}

    def run_stream(eng_obj, lst):
        hw = {}
        for op in lst:
            for sem_key, val in op.waits:
                if hw.get(sem_key, 0) >= val:
                    continue
                eng_obj.wait_ge(sems[sem_key], val)
                hw[sem_key] = val
            if op.issue_wait is not None:
                skey, val = op.issue_wait
                if hw.get(skey, 0) < val:
                    eng_obj.wait_ge(sems[skey], val)
                    hw[skey] = val
            n0 = nc.n_instructions()
            ins = op.fn(eng_obj)
            if op.queue:
                measured[op.idx] = nc.n_instructions() - n0
            if op.inc:
                ins.then_inc(sems[op.qsem if op.queue else op.eng],
                             16 if op.queue else 1)

    with nc.Block() as blk:
        @blk.tensor
        def _(e):
            run_stream(e, streams["T"])

        @blk.vector
        def _(e):
            run_stream(e, streams["V"])

        @blk.scalar
        def _(e):
            run_stream(e, streams["S"])

        @blk.gpsimd
        def _(e):
            run_stream(e, streams["P"])

        @blk.sync
        def _(e):
            run_stream(e, streams["Y"])

    if dma_counts is None:
        return measured
    return nc


_NC_CACHE = [None]


def kernel(**inputs):
    inp = {k: np.asarray(v) for k, v in inputs.items()}
    x = inp["L0_attn_x"].astype(np.float32, copy=False)

    wkT = np.ascontiguousarray(inp["L0_attn_Wk"].T.astype(np.float32, copy=False))
    wvT = np.ascontiguousarray(inp["L0_attn_Wv"].T.astype(np.float32, copy=False))
    wqT = np.ascontiguousarray(inp["L0_attn_Wq"].T.astype(np.float32, copy=False))
    woT = np.ascontiguousarray(inp["L0_attn_Wo"].T.astype(np.float32, copy=False))
    brows = np.ascontiguousarray(np.stack([
        inp["L0_attn_bk"], inp["L0_attn_bv"], inp["L0_attn_bq"],
        inp["L0_attn_bo"], inp["L0_ffn_fc2_bias"]]).astype(np.float32))
    gamf = np.ascontiguousarray(
        inp["L0_ffn_ln_gamma"].astype(np.float32).reshape(8, 128).T)
    betf = np.ascontiguousarray(
        inp["L0_ffn_ln_beta"].astype(np.float32).reshape(8, 128).T)
    b1f = np.ascontiguousarray(
        inp["L0_ffn_fc1_bias"].astype(np.float32).reshape(32, 128).T)
    bqf = np.ascontiguousarray(
        inp["L0_attn_bq"].astype(np.float32).reshape(8, 128).T)
    q1 = np.ascontiguousarray(
        inp["L0_ffn_fc1_q"].astype(np.uint8, copy=False)
        .reshape(8, 128, 8, 512).transpose(0, 2, 1, 3).reshape(64 * 128, 512))
    s1 = np.ascontiguousarray(inp["L0_ffn_fc1_scales"].astype(np.float32, copy=False))
    q2 = np.ascontiguousarray(
        inp["L0_ffn_fc2_q"].astype(np.uint8, copy=False)
        .reshape(32, 128, 2, 512).transpose(0, 2, 1, 3).reshape(64 * 128, 512))
    s2 = np.ascontiguousarray(inp["L0_ffn_fc2_scales"].astype(np.float32, copy=False))

    if _NC_CACHE[0] is None:
        _NC_CACHE[0] = build_nc()
    nc = _NC_CACHE[0]

    in_maps = []
    for c in range(N_CORES):
        xc = np.ascontiguousarray(x[:, c * 512:(c + 1) * 512, :].reshape(M, C))
        in_maps.append(dict(
            x=xc, xT=np.ascontiguousarray(xc.T),
            wkT=wkT, wvT=wvT, wqT=wqT, woT=woT, brows=brows,
            gamf=gamf, betf=betf, b1f=b1f, bqf=bqf,
            q1=q1, s1=s1, q2=q2, s2=s2))

    res = run_bass_kernel_spmd(nc, in_maps, core_ids=list(range(N_CORES)))
    out = np.empty((B, T, C), np.float32)
    for c in range(N_CORES):
        out[:, c * 512:(c + 1) * 512, :] = res.results[c]["y"].reshape(B, 512, C)
    return out
